# version 59
# baseline (speedup 1.0000x reference)
import contextlib
import heapq
import numpy as np

import concourse.bass as bass
import concourse.mybir as mybir
from concourse.bass_utils import run_bass_kernel_spmd

F32 = mybir.dt.float32
BF16 = mybir.dt.bfloat16
AF = mybir.ActivationFunctionType
ALU = mybir.AluOpType
AX = mybir.AxisListType

P = 128
D = 1024
H = 16
DH = 64
T = 2048
NT = 16
NCH = 4
CH = 8
QB = 128.0
EPS = 1e-5
LN_EPS_OUT = 1e-5
LN_EPS_IN = 1e-3
NEG = -1e30
KFW = P + T

ENGINES = ("sync", "tensor", "vector", "scalar", "gpsimd")
DEF_DUR = {"sync": 150, "tensor": 260, "vector": 500, "scalar": 450,
           "gpsimd": 900}


HOP = 420.0


class Plan:
    def __init__(self):
        self.ops = []
        self.last_w = {}
        self.readers = {}
        self._gid_auto = 0

    def op(self, eng, fn, r=(), w=(), dma=False, cc=False, deps=(),
           dsem=None, est=None, gid=None, solo=False):
        d = set(deps)
        if eng == "tensor" and not dma and not cc and gid is None:
            self._gid_auto += 1
            gid = ("auto", self._gid_auto)
        for k in r:
            if k in self.last_w:
                d.add(self.last_w[k])
        for k in w:
            if k in self.last_w:
                d.add(self.last_w[k])
            for rd in self.readers.get(k, ()):
                d.add(rd)
        i = len(self.ops)
        d.discard(i)
        for k in r:
            self.readers.setdefault(k, []).append(i)
        for k in w:
            self.last_w[k] = i
            self.readers[k] = []
        self.ops.append(dict(eng=eng, fn=fn, deps=d, dma=dma, cc=cc,
                             dsem=dsem, est=est, gid=gid, solo=solo))
        return i

    def schedule(self):
        ops = self.ops
        n = len(ops)
        succ = [[] for _ in range(n)]
        indeg = [0] * n
        for i, o in enumerate(ops):
            for d in o["deps"]:
                succ[d].append(i)
                indeg[i] += 1
        from collections import Counter
        is_pe_op = [o["eng"] == "tensor" and not o["dma"] and not o["cc"]
                    for o in ops]
        is_solo = [is_pe_op[i] and bool(o.get("solo")) for i, o in
                   enumerate(ops)]
        gid_of = [o["gid"] if (is_pe_op[i] and not is_solo[i]) else None
                  for i, o in enumerate(ops)]
        gsize = Counter(g for g in gid_of if g is not None)
        ext_rem = Counter()
        for i, o in enumerate(ops):
            if gid_of[i] is None:
                continue
            for d in o["deps"]:
                if gid_of[d] != gid_of[i]:
                    ext_rem[gid_of[i]] += 1
        pe_gid = [None]
        pe_rem = dict(gsize)
        def durf(i):
            o = ops[i]
            if o["dma"]:
                return o["est"] or 2200
            if o["cc"]:
                return 6000
            return o["est"] or DEF_DUR[o["eng"]]
        prio = [0.0] * n
        for i in range(n - 1, -1, -1):
            p = 0.0
            for s in succ[i]:
                hop = 0.0 if ops[s]["eng"] == ops[i]["eng"] else HOP
                p = max(p, prio[s] + hop)
            prio[i] = p + durf(i)
        ready = {e: set() for e in ENGINES}
        for i, o in enumerate(ops):
            if indeg[i] == 0:
                ready[o["eng"]].add(i)
        free = {e: 0.0 for e in ENGINES}
        comp = [0.0] * n
        rdy = [0.0] * n
        rdy_src = [-1] * n
        expw = [0.0] * n
        order = []

        def pick(e):
            h = ready[e]
            if not h:
                return None
            if e == "tensor":
                if pe_gid[0] is not None:
                    cand = [i for i in h
                            if is_solo[i] or gid_of[i] == pe_gid[0]]
                else:
                    cand = [i for i in h
                            if is_solo[i] or ext_rem[gid_of[i]] == 0]
                if not cand:
                    return None
            else:
                cand = h
            fe = free[e]
            return min(cand, key=lambda i: (max(fe, rdy[i]), -prio[i], i))

        for _step in range(n):
            best = None
            for e in ENGINES:
                i = pick(e)
                if i is None:
                    continue
                st = max(free[e], rdy[i])
                key = (st, -prio[i], i)
                if best is None or key < best[0]:
                    best = (key, i, e)
            if best is None:
                raise RuntimeError(
                    f"scheduler wedged at step {_step}: pe_gid={pe_gid[0]}")
            (st, _, _), i, e = best
            expw[i] = max(0.0, rdy[i] - free[e])
            ready[e].discard(i)
            if e == "tensor" and not is_solo[i]:
                g = gid_of[i]
                pe_rem[g] -= 1
                pe_gid[0] = g if pe_rem[g] > 0 else None
            o = ops[i]
            if o["dma"]:
                free[e] = st + (700 if e == "gpsimd" else 600)
                comp[i] = st + (o["est"] or 2200)
            elif o["cc"]:
                free[e] = st + 300
                comp[i] = st + 6000
            else:
                dur = o["est"] or DEF_DUR[e]
                free[e] = comp[i] = st + dur
            order.append(i)
            for s2 in succ[i]:
                hop = 0.0 if ops[s2]["eng"] == e else HOP
                if comp[i] + hop > rdy[s2]:
                    rdy[s2] = comp[i] + hop
                    rdy_src[s2] = i
                indeg[s2] -= 1
                if indeg[s2] == 0:
                    ready[ops[s2]["eng"]].add(s2)
                if gid_of[s2] is not None and gid_of[s2] != gid_of[i]:
                    ext_rem[gid_of[s2]] -= 1
        self.est_span = max(comp) if n else 0.0
        self.model_comp = comp
        import os
        if os.environ.get("KN_ANALYZE"):
            from collections import Counter as C2
            agg = C2()
            aggt = C2()
            for i in order:
                if expw[i] <= 0:
                    continue
                e = ops[i]["eng"]
                se = ops[rdy_src[i]]["eng"] if rdy_src[i] >= 0 else "-"
                sd = "dma" if (rdy_src[i] >= 0 and ops[rdy_src[i]]["dma"]) \
                    else ("cc" if (rdy_src[i] >= 0 and ops[rdy_src[i]]["cc"])
                          else se)
                agg[(e, sd)] += expw[i]
                aggt[int(comp[i] / 1e5)] += expw[i]
            print("[analyze] exposed wait by (engine <- source):")
            for k, v in agg.most_common(14):
                print(f"   {k[0]:7s} <- {k[1]:7s} : {v/1e3:8.1f} us")
            print("[analyze] exposed wait per 100us bucket:",
                  {k * 100: round(v / 1e3) for k, v in sorted(aggt.items())})
            prev_on_eng = {}
            prev_of = [-1] * n
            for i in order:
                e = ops[i]["eng"]
                prev_of[i] = prev_on_eng.get(e, -1)
                prev_on_eng[e] = i
            start = [comp[i] - durf(i) for i in range(n)]
            cur = max(range(n), key=lambda i: comp[i])
            attr = C2()
            pathn = C2()
            while cur >= 0:
                o = ops[cur]
                kind = ("dma" if o["dma"] else "cc" if o["cc"] else o["eng"])
                attr[("dur", kind)] += durf(cur)
                pathn[kind] += 1
                ds = rdy_src[cur]
                dep_t = rdy[cur]
                pe = prev_of[cur]
                eng_t = comp[pe] if pe >= 0 else 0.0
                if ds >= 0 and dep_t >= eng_t and dep_t >= start[cur] - 1:
                    hop = 0.0 if ops[ds]["eng"] == o["eng"] else HOP
                    attr[("hop", f"{ops[ds]['eng']}->{o['eng']}")] += hop
                    cur = ds
                elif pe >= 0:
                    attr[("busy", o["eng"])] += max(0.0, start[cur] - eng_t)
                    cur = pe
                else:
                    break
            print("[analyze] critical path attribution (us):")
            for k, v in attr.most_common(18):
                print(f"   {k[0]:5s} {k[1]:16s} : {v/1e3:8.1f}")
            print("[analyze] path op counts:", dict(pathn))
        return order

    def emit(self, nc, stack):
        import os
        tr = os.environ.get("KN_TRUNC")
        if tr:
            self.ops = self.ops[:int(tr)]
        ops = self.ops
        n = len(ops)
        def is_pe(o):
            return o["eng"] == "tensor" and not o["dma"] and not o["cc"]
        hard = [None] * n
        for i, o in enumerate(ops):
            if is_pe(o):
                hard[i] = {d for d in o["deps"] if not is_pe(ops[d])}
            else:
                hard[i] = o["deps"]
        import os
        if os.environ.get("KN_NOSCHED"):
            self.schedule()
            order = list(range(n))
        else:
            order = self.schedule()
        needed = set()
        for i in range(n):
            needed |= hard[i]
        sem_keys = set()
        for i, o in enumerate(ops):
            if o["dma"]:
                sem_keys.add(("d", o["dsem"]))
            elif o["cc"]:
                sem_keys.add(("c", 0))
            elif i in needed:
                sem_keys.add(("e", o["eng"]))
        sems = {k: stack.enter_context(nc.semaphore(f"s_{k[0]}_{k[1]}"))
                for k in sorted(sem_keys)}
        val = [None] * n
        cnt = {}
        for i in order:
            o = ops[i]
            if o["dma"]:
                k = ("d", o["dsem"])
                cnt[k] = cnt.get(k, 0) + 16
                val[i] = (k, cnt[k])
            elif o["cc"]:
                k = ("c", 0)
                cnt[k] = cnt.get(k, 0) + 1
                val[i] = (k, cnt[k])
            elif i in needed:
                k = ("e", o["eng"])
                cnt[k] = cnt.get(k, 0) + 1
                val[i] = (k, cnt[k])
        streams = {e: [] for e in ENGINES}
        wm = {e: {} for e in ENGINES}
        cover = [None] * n
        for i in order:
            o = ops[i]
            e = o["eng"]
            for dd in sorted(hard[i]):
                s, v = val[dd]
                if wm[e].get(s, 0) >= v:
                    continue
                streams[e].append(("wait", s, v))
                wm[e][s] = v
                for s2, v2 in cover[dd].items():
                    if wm[e].get(s2, 0) < v2:
                        wm[e][s2] = v2
            streams[e].append(("op", i))
            cover[i] = dict(wm[e])
            if val[i] is not None:
                s, v = val[i]
                cover[i][s] = max(cover[i].get(s, 0), v)

        def body_for(items):
            def body(eng):
                for it in items:
                    if it[0] == "wait":
                        eng.wait_ge(sems[it[1]], it[2])
                    else:
                        o = ops[it[1]]
                        ins = o["fn"](eng)
                        if val[it[1]] is not None:
                            s, v = val[it[1]]
                            ins.then_inc(sems[s], 16 if o["dma"] else 1)
            return body

        with nc.Block() as block:
            for ename in ENGINES:
                getattr(block, ename)(body_for(streams[ename]))


def build_nc():
    nc = bass.Bass("TRN2", target_bir_lowering=False)

    x_d = nc.dram_tensor("x", [P + T, D], F32, kind="ExternalInput").ap()
    wq_d = nc.dram_tensor("wq", [D, D], F32, kind="ExternalInput").ap()
    wk_d = nc.dram_tensor("wk", [D, D], F32, kind="ExternalInput").ap()
    wv_d = nc.dram_tensor("wv", [D, D], F32, kind="ExternalInput").ap()
    wo_d = nc.dram_tensor("wo", [D, D], F32, kind="ExternalInput").ap()
    wf_d = nc.dram_tensor("wf", [D, D], F32, kind="ExternalInput").ap()
    gam_d = nc.dram_tensor("gam", [1, D], F32, kind="ExternalInput").ap()
    bet_d = nc.dram_tensor("bet", [1, D], F32, kind="ExternalInput").ap()
    flag_d = nc.dram_tensor("flag", [P, 1], F32, kind="ExternalInput").ap()
    y_d = nc.dram_tensor("y", [T, D], F32, kind="ExternalOutput").ap()
    xnew_d = nc.dram_tensor("xnew", [T, D], F32).ap()
    gmx_d = nc.dram_tensor("gmx", [P, 1], F32)
    cc_in = nc.dram_tensor("cc_in", [1, 1], F32)
    cc_out = nc.dram_tensor("cc_out", [1, 1], F32)
    ccw_in = nc.dram_tensor("ccw_in", [1, 1], F32)
    ccw_out = nc.dram_tensor("ccw_out", [1, 1], F32)

    pl = Plan()

    with contextlib.ExitStack() as st:
        sb = lambda name, shape, dt: st.enter_context(nc.sbuf_tensor(name, shape, dt))
        ps = lambda name, shape, dt: st.enter_context(nc.psum_tensor(name, shape, dt))

        xbuf = sb("xbuf", [P, 2 * D], F32)
        wa = sb("wa", [P, CH * D], BF16)
        wb = sb("wb", [P, CH * D], BF16)
        wc = sb("wc", [P, CH * D], BF16)
        k_fm = sb("k_fm", [P, CH * KFW], BF16)
        v_tm = sb("v_tm", [P, 17 * D], BF16)
        lnfm = sb("lnfm", [P, CH * KFW], BF16)
        qfm = sb("qfm", [P, CH * 512], BF16)
        wst = sb("wst", [P, 2 * D], F32)
        ln0bf = sb("ln0bf", [P, D], BF16)
        tmp_ln = sb("tmp_ln", [P, 2 * D], F32)
        aosb = sb("aosb", [P, 2 * D], F32)
        pn = sb("pn", [P, 4 * P], BF16)
        pta = sb("pta", [P, 4 * P], BF16)
        ctxfm = sb("ctxfm", [P, CH * P], BF16)
        gbg = sb("gbg", [P, D], F32)
        gbb = sb("gbb", [P, D], F32)
        ident = sb("ident", [P, P], BF16)
        mask_r = sb("mask_r", [P, P], BF16)
        mask_0 = sb("mask_0", [P, P], BF16)
        mask_l = sb("mask_l", [P, P], BF16)
        mex = wst[:, 0:P]
        flag_sb = sb("flag_sb", [P, 1], F32)
        ones_rf = sb("ones_rf", [1, P], F32)
        ones_cf = sb("ones_cf", [P, 1], F32)
        kconst = sb("kconst", [P, 8], F32)
        cols = sb("cols", [P, 88], F32)
        bnst = sb("bnst", [P, 48], F32)
        scs = sb("scs", [1, 16], F32)

        s_ps = [ps(f"s_ps{i}", [P, 512], F32) for i in range(2)]
        tp_ps = [ps(f"tp_ps{i}", [P, P], BF16) for i in range(2)]
        cx_ps = [ps(f"cx_ps{i}", [P, P], F32) for i in range(2)]
        mm_ps = [ps(f"mm_ps{i}", [P, 512], F32) for i in range(2)]

        col = lambda i: cols[:, i:i + 1]
        ST_MEAN, ST_VAR, ST_SD, ST_RSTD, ST_NMR = range(5)
        stc = lambda par, i: cols[:, 8 * par + i:8 * par + i + 1]
        atc = lambda s, i: cols[:, 32 + 4 * s + i:32 + 4 * s + i + 1]
        C_GMAX, C_ABS, C_INVB, C_S1B, C_S2B, C_TMP, C_TMP2 = 80, 81, 82, 83, 84, 85, 86
        C_TMP3 = 79
        S_BETA, S_INV, S_GAM, S_S1, S_S2 = range(5)
        sc = lambda i: scs[:, i:i + 1]

        KEPS5, KEPS3, KNEG1, KM05 = 0, 1, 2, 3
        EPS_SLOT = {LN_EPS_OUT: KEPS5, LN_EPS_IN: KEPS3}
        kc_ap = lambda i: kconst[:, i:i + 1]

        tctr = [0]
        mctr = [0]

        pl.op("sync", lambda e: e.dma_start(out=flag_sb[:, :], in_=flag_d),
              w=["flag"], dma=True, dsem="flag", est=800)
        pl.op("gpsimd", lambda e: e.memset(ident[:, :], 0.0), w=["id"], est=150)
        pl.op("gpsimd", lambda e: e.affine_select(
            out=ident[:, :], in_=ident[:, :], compare_op=ALU.not_equal,
            fill=1.0, base=0, pattern=[[-1, P]], channel_multiplier=1),
            r=["id"], w=["id"], est=400)
        pl.op("gpsimd", lambda e: e.memset(mask_r[:, :], 1.0), w=["mr"], est=200)
        pl.op("gpsimd", lambda e: e.affine_select(
            out=mask_r[0:64, :], in_=mask_r[0:64, :], compare_op=ALU.is_ge,
            fill=0.0, base=0, pattern=[[1, P]], channel_multiplier=-1),
            r=["mr"], w=["mr"], est=400)
        pl.op("gpsimd", lambda e: e.affine_select(
            out=mask_r[0:64, :], in_=mask_r[0:64, :], compare_op=ALU.is_ge,
            fill=0.0, base=63, pattern=[[-1, P]], channel_multiplier=1),
            r=["mr"], w=["mr"], est=400)
        pl.op("sync", lambda e: e.dma_start(
            out=mask_r[64:128, :], in_=mask_r[0:64, :]),
            r=["mr"], w=["mr"], dma=True, dsem="mcp", est=800)
        for i, v in enumerate([1e-5, 1e-3, -1.0, -0.5, 0.125]):
            pl.op("gpsimd", lambda e, i=i, v=v: e.memset(kconst[:, i:i + 1], v),
                  w=[("kc", i)], est=100)
        pl.op("vector", lambda e: e.tensor_scalar(
            out=kconst[:, 5:6], in0=flag_sb[:, 0:1], scalar1=-1.0,
            scalar2=1.0, op0=ALU.mult, op1=ALU.add),
            r=["flag"], w=[("kc", 5)], est=120)
        pl.op("gpsimd", lambda e: e.memset(mex[:, :], 1.0), w=[("wst", 0)], est=200)
        pl.op("gpsimd", lambda e: e.affine_select(
            out=mex[:, :], in_=mex[:, :], compare_op=ALU.is_ge,
            fill=0.0, base=-63, pattern=[[1, P]], channel_multiplier=0),
            r=[("wst", 0)], w=[("wst", 0)], est=400)
        pl.op("vector", lambda e: e.tensor_scalar(
            out=mex[:, :], in0=mex[:, :], scalar1=kc_ap(5), scalar2=1.0,
            op0=ALU.add, op1=ALU.min),
            r=[("wst", 0), ("kc", 5)], w=[("wst", 0)], est=300)
        pl.op("vector", lambda e: e.tensor_tensor(
            out=mask_0[:, :], in0=mex[:, :], in1=mask_r[:, :], op=ALU.mult),
            r=[("wst", 0), "mr"], w=["m0"], est=350)
        pl.op("vector", lambda e: e.tensor_copy(
            out=mask_l[:, :], in_=mask_r[:, :]), r=["mr"], w=["ml"], est=250)
        pl.op("gpsimd", lambda e: e.memset(mask_l[:, 127:128], 0.0),
              r=["ml"], w=["ml"], est=100)
        pl.op("gpsimd", lambda e: e.memset(ones_rf[:1, :], 1.0), w=["onrf"], est=100)
        pl.op("vector", lambda e: e.memset(col(C_GMAX), 0.0),
              w=[("c", C_GMAX)], est=100)
        pl.op("gpsimd", lambda e: e.memset(ones_cf[:, :], 1.0), w=["oncf"], est=100)
        pl.op("gpsimd", lambda e: e.memset(v_tm[:, 16 * D:17 * D], 0.0),
              w=[("v", 16, 0), ("v", 16, 1)], est=600)
        pl.op("gpsimd", lambda e: e.dma_start(
            out=ccw_in.ap(), in_=ones_rf[0:1, 0:1]),
            r=["onrf"], dma=True, dsem="ccww", est=1200)
        pl.op("gpsimd", lambda e: e.collective_compute(
            "AllReduce", ALU.max, replica_groups=[list(range(8))],
            ins=[ccw_in.ap()], outs=[ccw_out.ap()]),
            deps=[len(pl.ops) - 1], cc=True)

        wctr = [0]

        def load_weight_bf16(wdram, dst, dst_key):
            for kc in range(CH):
                s4 = wctr[0] % 4
                wctr[0] += 1
                if s4 < 2:
                    stg = wst[:, s4 * D:(s4 + 1) * D]
                    skeys = [("wst", s4)]
                else:
                    stg = tmp_ln[:, (s4 - 2) * D:(s4 - 1) * D]
                    skeys = [("tl", s4 - 2, 0), ("tl", s4 - 2, 1)]
                pl.op("sync", lambda e, stg=stg, kc=kc, wdram=wdram: e.dma_start(
                    out=stg, in_=wdram[kc * P:(kc + 1) * P, :]),
                    w=skeys, dma=True, dsem=f"wst{s4}", est=2400)
                pl.op("scalar", lambda e, stg=stg, kc=kc: e.copy(
                    out=dst[:, kc * D:(kc + 1) * D], in_=stg),
                    r=skeys, w=[(dst_key, kc)], est=1100)

        def ln_stats(src_ap, src_keys, eps, par, eng2="vector"):
            b0 = 12 * par
            pl.op("vector", lambda e: e.bn_stats(
                out=bnst[:, b0:b0 + 6], in_=src_ap[:, 0:512]),
                r=src_keys, w=[("bn", par)], est=620)
            pl.op(eng2, lambda e: e.bn_stats(
                out=bnst[:, b0 + 6:b0 + 12], in_=src_ap[:, 512:1024]),
                r=src_keys, w=[("bn2", par)],
                est=620 if eng2 == "vector" else 1000)
            pl.op("vector", lambda e: e.bn_aggr(
                out=cols[:, 8 * par:8 * par + 2], in_=bnst[:, b0:b0 + 12]),
                r=[("bn", par), ("bn2", par)],
                w=[("st", par, ST_MEAN), ("st", par, ST_VAR)], est=150)
            es = EPS_SLOT[eps]
            pl.op("scalar", lambda e, es=es, par=par: e.activation(
                out=stc(par, ST_SD), in_=stc(par, ST_VAR), func=AF.Ln,
                bias=kc_ap(es), scale=1.0),
                r=[("st", par, ST_VAR), ("kc", es)], w=[("st", par, ST_SD)],
                est=250)
            pl.op("scalar", lambda e, par=par: e.activation(
                out=stc(par, ST_RSTD), in_=stc(par, ST_SD), func=AF.Exp,
                bias=0.0, scale=kc_ap(KM05)),
                r=[("st", par, ST_SD), ("kc", KM05)],
                w=[("st", par, ST_RSTD)], est=250)
            pl.op("vector", lambda e, par=par: e.tensor_scalar(
                out=stc(par, ST_NMR), in0=stc(par, ST_MEAN),
                scalar1=stc(par, ST_RSTD), scalar2=-1.0,
                op0=ALU.mult, op1=ALU.mult),
                r=[("st", par, ST_MEAN), ("st", par, ST_RSTD)],
                w=[("st", par, ST_NMR)], est=170)

        def ln_apply(dst_ap, dst_keys, src_ap, src_keys, par, eng="gpsimd",
                     est=1250):
            pl.op(eng, lambda e: e.tensor_scalar(
                out=dst_ap, in0=src_ap, scalar1=stc(par, ST_RSTD),
                scalar2=stc(par, ST_NMR), op0=ALU.mult, op1=ALU.add),
                r=list(src_keys) + [("st", par, ST_RSTD), ("st", par, ST_NMR)],
                w=list(dst_keys), est=est)

        HD = D // 2

        def ln_stats_sc(src_ap, src_keys, eps, par):
            b = par % 2
            c0 = 60 + 4 * par
            for h in range(2):
                pl.op("scalar", lambda e, h=h: e.activation(
                    out=s_ps[b][:, 0:512], in_=src_ap[:, h * 512:(h + 1) * 512],
                    func=AF.Copy, bias=0.0, scale=1.0, accum_out=col(c0 + h)),
                    r=list(src_keys), w=[("sp", b), ("c", c0 + h)], est=740)
                pl.op("scalar", lambda e, h=h: e.activation(
                    out=s_ps[b][:, 0:512], in_=src_ap[:, h * 512:(h + 1) * 512],
                    func=AF.Square, bias=0.0, scale=1.0,
                    accum_out=col(c0 + 2 + h)),
                    r=list(src_keys), w=[("sp", b), ("c", c0 + 2 + h)], est=740)
            pl.op("vector", lambda e: e.tensor_tensor(
                out=stc(par, ST_MEAN), in0=col(c0), in1=col(c0 + 1),
                op=ALU.add), r=[("c", c0), ("c", c0 + 1)],
                w=[("st", par, ST_MEAN)], est=120)
            pl.op("vector", lambda e: e.tensor_scalar(
                out=stc(par, ST_MEAN), in0=stc(par, ST_MEAN),
                scalar1=1.0 / D, scalar2=None, op0=ALU.mult),
                r=[("st", par, ST_MEAN)], w=[("st", par, ST_MEAN)], est=120)
            pl.op("vector", lambda e: e.tensor_tensor(
                out=stc(par, ST_VAR), in0=col(c0 + 2), in1=col(c0 + 3),
                op=ALU.add), r=[("c", c0 + 2), ("c", c0 + 3)],
                w=[("st", par, ST_VAR)], est=120)
            pl.op("vector", lambda e: e.tensor_scalar(
                out=col(c0), in0=stc(par, ST_MEAN),
                scalar1=stc(par, ST_MEAN), scalar2=None, op0=ALU.mult),
                r=[("st", par, ST_MEAN)], w=[("c", c0)], est=120)
            pl.op("vector", lambda e: e.scalar_tensor_tensor(
                out=stc(par, ST_VAR), in0=stc(par, ST_VAR), scalar=1.0 / D,
                in1=col(c0), op0=ALU.mult, op1=ALU.subtract),
                r=[("st", par, ST_VAR), ("c", c0)], w=[("st", par, ST_VAR)],
                est=120)
            es = EPS_SLOT[eps]
            pl.op("scalar", lambda e, es=es, par=par: e.activation(
                out=stc(par, ST_SD), in_=stc(par, ST_VAR), func=AF.Ln,
                bias=kc_ap(es), scale=1.0),
                r=[("st", par, ST_VAR), ("kc", es)], w=[("st", par, ST_SD)],
                est=250)
            pl.op("scalar", lambda e, par=par: e.activation(
                out=stc(par, ST_RSTD), in_=stc(par, ST_SD), func=AF.Exp,
                bias=0.0, scale=kc_ap(KM05)),
                r=[("st", par, ST_SD), ("kc", KM05)],
                w=[("st", par, ST_RSTD)], est=250)
            pl.op("vector", lambda e, par=par: e.tensor_scalar(
                out=stc(par, ST_NMR), in0=stc(par, ST_MEAN),
                scalar1=stc(par, ST_RSTD), scalar2=-1.0,
                op0=ALU.mult, op1=ALU.mult),
                r=[("st", par, ST_MEAN), ("st", par, ST_RSTD)],
                w=[("st", par, ST_NMR)], est=170)

        def ln_apply_sp(dst_ap, dkeys_of, src_ap, skeys_of, par):
            for h, eng, es in ((0, "vector", 700), (1, "gpsimd", 1300)):
                c0, c1 = h * HD, (h + 1) * HD
                pl.op(eng, lambda e, c0=c0, c1=c1: e.tensor_scalar(
                    out=dst_ap[:, c0:c1], in0=src_ap[:, c0:c1],
                    scalar1=stc(par, ST_RSTD), scalar2=stc(par, ST_NMR),
                    op0=ALU.mult, op1=ALU.add),
                    r=list(skeys_of(h)) + [("st", par, ST_RSTD),
                                           ("st", par, ST_NMR)],
                    w=list(dkeys_of(h)), est=es)

        def transpose128(src_ap, src_keys, dst_ap, dst_keys, width=P,
                         evac_eng="vector", bank=None):
            if bank is None:
                s = tctr[0] % 2
                tctr[0] += 1
            else:
                s = bank
            pl.op("tensor", lambda e, s=s: e.transpose(
                tp_ps[s][0:width, :], src_ap, ident[:, :]),
                r=list(src_keys) + ["id"], w=[("tp", s)], est=230,
                solo=True)
            if evac_eng == "vector":
                pl.op("vector", lambda e, s=s: e.tensor_copy(
                    out=dst_ap, in_=tp_ps[s][0:width, :]),
                    r=[("tp", s)], w=list(dst_keys), est=200)
            else:
                pl.op("scalar", lambda e, s=s: e.copy(
                    out=dst_ap, in_=tp_ps[s][0:width, :]),
                    r=[("tp", s)], w=list(dst_keys), est=250)

        def mm_group(lhsT_of_kc, rhs_of_kc, rkeys_of_kc, evac_fn, evac_rextra,
                     evac_w, n_k=CH, evac_eng="scalar", evac_est=600,
                     mm_est=310, out_rows=P, out_cols=512):
            sl = mctr[0] % 2
            g = ("mm", mctr[0])
            mctr[0] += 1
            for kc in range(n_k):
                pl.op("tensor", lambda e, kc=kc, sl=sl: e.matmul(
                    mm_ps[sl][0:out_rows, 0:out_cols], lhsT_of_kc(kc),
                    rhs_of_kc(kc),
                    start=(kc == 0), stop=(kc == n_k - 1),
                    skip_group_check=True),
                    r=rkeys_of_kc(kc), w=[("mm", sl)], est=mm_est, gid=g)
            pl.op(evac_eng, lambda e, sl=sl: evac_fn(e, mm_ps[sl][0:out_rows, :]),
                  r=[("mm", sl)] + list(evac_rextra), w=list(evac_w),
                  est=evac_est)

        def ln0_tile(row0, xslot, par):
            if xslot < 2:
                xa = xbuf[:, xslot * D:(xslot + 1) * D]
                xkeys = [("xb", xslot)]
            else:
                xa = aosb[:, (xslot - 2) * D:(xslot - 1) * D]
                xkeys = [("ao", xslot - 2, 0), ("ao", xslot - 2, 1)]
            pl.op("sync", lambda e, xa=xa, row0=row0: e.dma_start(
                out=xa, in_=x_d[row0:row0 + P, :]),
                w=xkeys, dma=True, dsem=f"xb{xslot}", est=2400)
            ln_stats(xa, xkeys, LN_EPS_OUT, par)
            if xslot % 2 == 0:
                la = ln0bf[:, 0:D]
                lk_of = lambda h: [("l0", h)]
            else:
                la = ctxfm[:, 0:D]
                lk_of = lambda h: [("cfm", m, j)
                                   for m in range(4 * h, 4 * h + 4)
                                   for j in range(2)]
            ln_apply_sp(la, lk_of, xa, lambda h: xkeys, par)
            return la, lk_of

        grow = wst[0:1, 0:D]
        brow = wst[0:1, D:2 * D]
        pl.op("sync", lambda e: e.dma_start(out=grow, in_=gam_d),
              w=[("wst", 0)], dma=True, dsem="grow", est=800)
        pl.op("sync", lambda e: e.dma_start(out=brow, in_=bet_d),
              w=[("wst", 1)], dma=True, dsem="brow", est=800)
        for nh in range(2):
            pl.op("tensor", lambda e, nh=nh: e.matmul(
                mm_ps[0][:, :], ones_rf[:1, :], grow[:, nh * 512:(nh + 1) * 512],
                start=True, stop=True, skip_group_check=True),
                r=["onrf", ("wst", 0)], w=[("mm", 0)], est=500,
                solo=True)
            pl.op("vector", lambda e, nh=nh: e.tensor_copy(
                out=gbg[:, nh * 512:(nh + 1) * 512], in_=mm_ps[0][:, :]),
                r=[("mm", 0)], w=["gbg"], est=650)
            pl.op("tensor", lambda e, nh=nh: e.matmul(
                mm_ps[1][:, :], ones_rf[:1, :], brow[:, nh * 512:(nh + 1) * 512],
                start=True, stop=True, skip_group_check=True),
                r=["onrf", ("wst", 1)], w=[("mm", 1)], est=500,
                solo=True)
            pl.op("vector", lambda e, nh=nh: e.tensor_copy(
                out=gbb[:, nh * 512:(nh + 1) * 512], in_=mm_ps[1][:, :]),
                r=[("mm", 1)], w=["gbb"], est=650)


        load_weight_bf16(wk_d, wa, "wa")
        load_weight_bf16(wv_d, wb, "wb")

        def lnfm_tile(gt):
            par = (gt + 2) % 4
            la, lk_of = ln0_tile((gt + 1) * P, (gt + 1) % 4, par)
            base = (gt + 1) * P
            for kc in range(CH):
                transpose128(la[:, kc * P:(kc + 1) * P], lk_of(kc // 4),
                             lnfm[:, kc * KFW + base:kc * KFW + base + P],
                             [("lf", kc, gt)])

        lnfm_tile(-1)
        for m in range(CH):
            mm_group(
                lambda kc, m=m: wa[:, kc * D + m * P:kc * D + (m + 1) * P],
                lambda kc: lnfm[:, kc * KFW:kc * KFW + P],
                lambda kc, m=m: [("wa", kc), ("lf", kc, -1)],
                lambda e, src, m=m: e.copy(
                    out=k_fm[:, m * KFW:m * KFW + P], in_=src[:, 0:P]),
                [], [("k", m, -1)], evac_est=300, mm_est=110, out_cols=P)

        for c in range(NCH):
            for tq in range(4):
                lnfm_tile(4 * c + tq)
            for m in range(CH):
                mm_group(
                    lambda kc, m=m: wa[:, kc * D + m * P:kc * D + (m + 1) * P],
                    lambda kc, c=c: lnfm[:, kc * KFW + 512 * c + P:
                                         kc * KFW + 512 * c + 640],
                    lambda kc, c=c: [("wa", kc)] + [("lf", kc, t)
                                                    for t in range(4 * c, 4 * c + 4)],
                    lambda e, src, m=m, c=c: e.copy(
                        out=k_fm[:, m * KFW + 512 * c + P:
                                 m * KFW + 512 * c + 640], in_=src),
                    [], [("k", m, c)], evac_est=600)
            us = list(range(4 * c, 4 * c + 4)) + ([16] if c == 3 else [])
            for u in us:
                wdt = 63 if u == 16 else P
                for nh in range(2):
                    mm_group(
                        lambda kc, u=u, wdt=wdt: lnfm[:, kc * KFW + u * P + 65:
                                                      kc * KFW + u * P + 65 + wdt],
                        lambda kc, nh=nh: wb[:, kc * D + nh * 512:
                                             kc * D + (nh + 1) * 512],
                        lambda kc, u=u: [("wb", kc), ("lf", kc, u - 1),
                                         ("lf", kc, min(u, NT - 1))],
                        lambda e, src, u=u, nh=nh, wdt=wdt: e.copy(
                            out=v_tm[0:wdt, u * D + nh * 512:
                                     u * D + (nh + 1) * 512], in_=src),
                        [], [("v", u, nh)], out_rows=wdt, evac_est=600)

        load_weight_bf16(wq_d, wc, "wc")
        load_weight_bf16(wo_d, wb, "wb")

        for c in range(NCH):
            for m in range(CH):
                mm_group(
                    lambda kc, m=m: wc[:, kc * D + m * P:kc * D + (m + 1) * P],
                    lambda kc, c=c: lnfm[:, kc * KFW + 512 * c + P:
                                         kc * KFW + 512 * c + 640],
                    lambda kc, c=c: [("wc", kc)] + [("lf", kc, t)
                                                    for t in range(4 * c, 4 * c + 4)],
                    lambda e, src, m=m: e.activation(
                        out=qfm[:, m * 512:(m + 1) * 512], in_=src,
                        func=AF.Copy, scale=kc_ap(4)),
                    [("kc", 4)], [("q", m)], evac_est=600)

            def emit_tile_tail(c, g):
                par = g % 2
                for nh in range(2):
                    mm_group(
                        lambda m2: ctxfm[:, m2 * P:(m2 + 1) * P],
                        lambda m2, nh=nh: wb[:, m2 * D + nh * 512:
                                             m2 * D + (nh + 1) * 512],
                        lambda m2: [("wb", m2), ("cfm", m2, 0), ("cfm", m2, 1)],
                        lambda e, src, nh=nh, par=par: e.copy(
                            out=aosb[:, par * D + nh * 512:
                                     par * D + (nh + 1) * 512], in_=src),
                        [], [("ao", par, nh)], evac_est=600)
                ao = aosb[:, par * D:(par + 1) * D]
                last = g == NT - 1
                tl = tmp_ln[:, par * D:(par + 1) * D]
                tlk = lambda h: [("tl", par, h)]
                tlk2 = [("tl", par, 0), ("tl", par, 1)]
                aok = lambda h: [("ao", par, h)]

                def big(build, rk_of, wk_of, eng="gpsimd", est=2400):
                    if not last:
                        pl.op(eng, lambda e: build(e, 0, D),
                              r=rk_of(0) + rk_of(1), w=wk_of(0) + wk_of(1),
                              est=est)
                    else:
                        for h, e2, es in ((0, "vector", est * 45 // 100),
                                          (1, "gpsimd", est * 55 // 100)):
                            pl.op(e2, lambda e, h=h: build(e, h * HD,
                                                           (h + 1) * HD),
                                  r=rk_of(h), w=wk_of(h), est=es)

                lnst = ln_stats_sc if last else ln_stats
                ln_stats(ao, [("ao", par, 0), ("ao", par, 1)], LN_EPS_OUT,
                         par)
                big(lambda e, c0, c1: e.tensor_scalar(
                    out=tl[:, c0:c1], in0=ao[:, c0:c1],
                    scalar1=stc(par, ST_RSTD), scalar2=stc(par, ST_NMR),
                    op0=ALU.mult, op1=ALU.add),
                    lambda h: aok(h) + [("st", par, ST_RSTD),
                                        ("st", par, ST_NMR)],
                    tlk, est=1400)
                xsl = g % 2
                xa = xbuf[:, xsl * D:(xsl + 1) * D]
                pl.op("sync", lambda e, g=g, xa=xa: e.dma_start(
                    out=xa, in_=x_d[(g + 1) * P:(g + 2) * P, :]),
                    w=[("xb", xsl)], dma=True, dsem=f"xb{xsl}", est=2400)
                pl.op("vector" if last else "gpsimd",
                      lambda e, xa=xa, tl=tl, g=g: e.tensor_tensor(
                          out=xa, in0=xa, in1=tl, op=ALU.add),
                      r=[("xb", xsl)] + tlk2, w=[("xb", xsl)],
                      est=1300 if last else 2400)
                pl.op("gpsimd", lambda e, g=g, xa=xa: e.dma_start(
                    out=xnew_d[g * P:(g + 1) * P, :], in_=xa),
                    r=[("xb", xsl)], w=[("xnd", g), "xnwseq"], dma=True,
                    dsem="xnw", est=2400)
                bpar = 2 + par
                lnst(xa, [("xb", xsl)], LN_EPS_OUT, bpar)
                big(lambda e, c0, c1: e.tensor_scalar(
                    out=tl[:, c0:c1], in0=xa[:, c0:c1],
                    scalar1=stc(bpar, ST_RSTD), scalar2=stc(bpar, ST_NMR),
                    op0=ALU.mult, op1=ALU.add),
                    lambda h: [("xb", xsl), ("st", bpar, ST_RSTD),
                               ("st", bpar, ST_NMR)],
                    tlk, est=1400)
                lnst(tl, tlk2, LN_EPS_IN, bpar)
                ao2 = aosb[:, par * D:(par + 1) * D]
                big(lambda e, c0, c1: e.tensor_scalar(
                    out=ao2[:, c0:c1], in0=tl[:, c0:c1],
                    scalar1=stc(bpar, ST_RSTD), scalar2=stc(bpar, ST_NMR),
                    op0=ALU.mult, op1=ALU.add),
                    lambda h: tlk(h) + [("st", bpar, ST_RSTD),
                                        ("st", bpar, ST_NMR)],
                    aok, est=1400)
                big(lambda e, c0, c1: e.tensor_tensor(
                    out=tl[:, c0:c1], in0=ao2[:, c0:c1], in1=gbg[:, c0:c1],
                    op=ALU.mult),
                    lambda h: aok(h) + ["gbg"], tlk, est=2400)
                big(lambda e, c0, c1: e.tensor_tensor(
                    out=ln0bf[:, c0:c1], in0=tl[:, c0:c1], in1=gbb[:, c0:c1],
                    op=ALU.add),
                    lambda h: tlk(h) + ["gbb"], lambda h: [("l0", h)],
                    est=2400)
                if not last:
                    pl.op("vector", lambda e: e.tensor_reduce(
                        out=col(C_TMP2), in_=ln0bf[:, 0:D], axis=AX.X,
                        op=ALU.max, apply_absolute_value=True),
                        r=[("l0", 0), ("l0", 1)], w=[("c", C_TMP2)], est=1200)
                    pl.op("vector", lambda e: e.tensor_tensor(
                        out=col(C_GMAX), in0=col(C_GMAX), in1=col(C_TMP2),
                        op=ALU.max),
                        r=[("c", C_GMAX), ("c", C_TMP2)], w=[("c", C_GMAX)],
                        est=170)
                else:
                    for h, cdst in ((0, C_TMP2), (1, C_TMP3)):
                        pl.op("vector", lambda e, h=h, cdst=cdst:
                              e.tensor_reduce(
                                  out=col(cdst),
                                  in_=ln0bf[:, h * HD:(h + 1) * HD],
                                  axis=AX.X, op=ALU.max,
                                  apply_absolute_value=True),
                              r=[("l0", h)], w=[("c", cdst)], est=650)
                    for csrc in (C_TMP2, C_TMP3):
                        pl.op("vector", lambda e, csrc=csrc: e.tensor_tensor(
                            out=col(C_GMAX), in0=col(C_GMAX), in1=col(csrc),
                            op=ALU.max),
                            r=[("c", C_GMAX), ("c", csrc)], w=[("c", C_GMAX)],
                            est=170)
                for kc in range(CH):
                    transpose128(ln0bf[:, kc * P:(kc + 1) * P],
                                 [("l0", kc // 4)],
                                 v_tm[:, g * D + kc * P:g * D + (kc + 1) * P],
                                 [("v", g, kc // 4)], evac_eng="scalar",
                                 bank=1)

            units = [(2 * (4 * c + tq) + uu, pm)
                     for tq in range(4) for uu in range(2) for pm in range(CH)]

            def st_scores(t2, pm, uid):
                sl = uid % 2
                sgid = ("sc", t2, pm)
                spt = s_ps[sl]
                qc0 = pm * 512 + (t2 - 8 * c) * 64
                kc0 = pm * KFW + t2 * 64 + 65
                kw = 127 if t2 == 31 else P
                for hp in range(2):
                    pl.op("tensor", lambda e, hp=hp, pm=pm, spt=spt,
                          qc0=qc0, kc0=kc0, kw=kw: e.matmul(
                              spt[hp * 64:hp * 64 + 64, 0:kw],
                              qfm[hp * 64:hp * 64 + 64, qc0:qc0 + 64],
                              k_fm[hp * 64:hp * 64 + 64, kc0:kc0 + kw],
                              start=True, stop=True,
                              skip_group_check=True),
                          r=[("q", pm)] + [("k", pm, cc2) for cc2 in
                                           range(-1, t2 // 8 + 1)],
                          w=[("sp", sl)], est=240, gid=sgid, solo=True)

            def st_soft(t2, pm, uid):
                sl = uid % 2
                s4 = uid % 4
                spt = s_ps[sl]
                if t2 == 0:
                    mask_ap, mkey = mask_0, "m0"
                elif t2 == 31:
                    mask_ap, mkey = mask_l, "ml"
                else:
                    mask_ap, mkey = mask_r, "mr"
                pl.op("scalar", lambda e, s4=s4, spt=spt: e.activation(
                    out=pn[:, s4 * P:(s4 + 1) * P], in_=spt[:, 0:P],
                    func=AF.Exp, bias=0.0, scale=1.0),
                    r=[("sp", sl)], w=[("pn", s4)], est=420)
                pl.op("vector", lambda e, s4=s4, mask_ap=mask_ap:
                      e.scalar_tensor_tensor(
                          out=pn[:, s4 * P:(s4 + 1) * P],
                          in0=pn[:, s4 * P:(s4 + 1) * P], scalar=1.0,
                          in1=mask_ap[:, :], op0=ALU.mult, op1=ALU.mult,
                          accum_out=atc(s4, 1)),
                      r=[("pn", s4), mkey], w=[("pn", s4), ("at", s4, 1)],
                      est=320)

            def st_norm(t2, pm, uid):
                s4 = uid % 4
                pl.op("vector", lambda e, s4=s4: e.reciprocal(
                    out=atc(s4, 2), in_=atc(s4, 1)),
                    r=[("at", s4, 1)], w=[("at", s4, 2)], est=170)
                pl.op("vector", lambda e, s4=s4: e.tensor_scalar(
                    out=pn[:, s4 * P:(s4 + 1) * P],
                    in0=pn[:, s4 * P:(s4 + 1) * P],
                    scalar1=atc(s4, 2), scalar2=None, op0=ALU.mult),
                    r=[("pn", s4), ("at", s4, 2)], w=[("pn", s4)],
                    est=180)

            def st_tr(t2, pm, uid):
                s4 = uid % 4
                u2 = t2 % 2
                if u2 == 0:
                    transpose128(pn[:, s4 * P:(s4 + 1) * P],
                                 [("pn", s4)],
                                 pta[:, s4 * P:(s4 + 1) * P],
                                 [("pta", s4)], evac_eng="scalar", bank=0)
                else:
                    ts_ = 0
                    pl.op("tensor", lambda e, s4=s4, ts_=ts_: e.transpose(
                        tp_ps[ts_][64:128, :],
                        pn[:, s4 * P:s4 * P + 64], ident[:, :]),
                        r=[("pn", s4), "id"], w=[("tp", ts_)],
                        est=170, gid=("tr", t2, pm, 0), solo=True)
                    pl.op("tensor", lambda e, s4=s4, ts_=ts_: e.transpose(
                        tp_ps[ts_][0:64, :],
                        pn[:, s4 * P + 64:s4 * P + P], ident[:, :]),
                        r=[("pn", s4), "id"], w=[("tp", ts_)],
                        est=170, gid=("tr", t2, pm, 1), solo=True)
                    pl.op("scalar", lambda e, s4=s4, ts_=ts_: e.copy(
                        out=pta[:, s4 * P:(s4 + 1) * P],
                        in_=tp_ps[ts_][:, :]),
                        r=[("tp", ts_)], w=[("pta", s4)], est=250)

            def st_ctx(t2, pm, uid):
                s4 = uid % 4
                u2 = t2 % 2
                cps = uid % 2
                cgid = ("cx", t2, pm)
                u = t2 // 2
                for hp in range(2):
                    h = 2 * pm + hp
                    if u2 == 0:
                        pl.op("tensor", lambda e, hp=hp, h=h, u=u,
                              s4=s4, cps=cps: e.matmul(
                                  cx_ps[cps][hp * 64:hp * 64 + 64, 0:64],
                                  v_tm[:, u * D + h * DH:
                                       u * D + (h + 1) * DH],
                                  pta[:, s4 * P + hp * 64:
                                      s4 * P + hp * 64 + 64],
                                  start=True, stop=True,
                                  skip_group_check=True),
                              r=[("v", u, h // 8), ("pta", s4)],
                              w=[("ctxp", cps)], est=180, gid=cgid,
                              solo=True)
                    else:
                        pl.op("tensor", lambda e, hp=hp, h=h, u=u,
                              s4=s4, cps=cps: e.matmul(
                                  cx_ps[cps][hp * 64:hp * 64 + 64, 0:64],
                                  v_tm[64:128, u * D + h * DH:
                                       u * D + (h + 1) * DH],
                                  pta[64:128, s4 * P + hp * 64:
                                      s4 * P + hp * 64 + 64],
                                  start=True, stop=True,
                                  skip_group_check=True),
                              r=[("v", u, h // 8), ("pta", s4)],
                              w=[("ctxp", cps)], est=180, gid=cgid,
                              solo=True)
                        pl.op("tensor", lambda e, hp=hp, h=h, u=u,
                              s4=s4, cps=cps: e.matmul(
                                  cx_ps[1 - cps][hp * 64:hp * 64 + 64, 0:64],
                                  v_tm[0:64, (u + 1) * D + h * DH:
                                       (u + 1) * D + (h + 1) * DH],
                                  pta[0:64, s4 * P + hp * 64:
                                      s4 * P + hp * 64 + 64],
                                  start=True, stop=True,
                                  skip_group_check=True),
                              r=[("v", u + 1, h // 8), ("pta", s4)],
                              w=[("ctxp", 1 - cps)], est=180, gid=cgid,
                              solo=True)

            def st_evac(t2, pm, uid):
                u2 = t2 % 2
                cps = uid % 2
                if u2 == 0:
                    pl.op("scalar", lambda e, pm=pm, cps=cps: e.copy(
                        out=ctxfm[:, pm * P:pm * P + 64],
                        in_=cx_ps[cps][:, 0:64]),
                        r=[("ctxp", cps)],
                        w=[("cfm", pm, 0)], est=200)
                else:
                    pl.op("scalar", lambda e, pm=pm, cps=cps: e.copy(
                        out=ctxfm[:, pm * P + 64:pm * P + 128],
                        in_=cx_ps[cps][:, 0:64]),
                        r=[("ctxp", cps)],
                        w=[("cfm", pm, 1)], est=200)
                    pl.op("vector", lambda e, pm=pm, cps=cps: e.tensor_tensor(
                        out=ctxfm[:, pm * P + 64:pm * P + 128],
                        in0=cx_ps[1 - cps][:, 0:64],
                        in1=ctxfm[:, pm * P + 64:pm * P + 128],
                        op=ALU.add),
                        r=[("ctxp", 1 - cps), ("cfm", pm, 1)],
                        w=[("cfm", pm, 1)], est=220)

            stages = [(5, st_evac), (4, st_ctx), (3, st_tr), (2, st_norm),
                      (1, st_soft), (0, st_scores)]
            NU = len(units)
            for step in range(NU + 6):
                ui_done = step - 6
                if 0 <= ui_done < NU and ui_done % 16 == 15:
                    emit_tile_tail(c, 4 * c + ui_done // 16)
                for lag, fn_ in stages:
                    ui = step - lag
                    if 0 <= ui < NU:
                        t2, pm = units[ui]
                        fn_(t2, pm, ui)
        pl.op("vector", lambda e: e.memset(col(C_ABS), 0.0), w=[("c", C_ABS)],
              est=100)
        for kc in range(CH):
            sl = kc % 2
            stg = wst[:, sl * D:(sl + 1) * D]
            pl.op("sync", lambda e, stg=stg, kc=kc: e.dma_start(
                out=stg, in_=wf_d[kc * P:(kc + 1) * P, :]),
                w=[("wst", sl)], dma=True, dsem=f"wst{sl}", est=2400)
            pl.op("vector", lambda e, stg=stg: e.tensor_reduce(
                out=col(C_TMP), in_=stg, axis=AX.X, op=ALU.add,
                apply_absolute_value=True),
                r=[("wst", sl)], w=[("c", C_TMP)], est=1200)
            pl.op("vector", lambda e: e.tensor_tensor(
                out=col(C_ABS), in0=col(C_ABS), in1=col(C_TMP), op=ALU.add),
                r=[("c", C_ABS), ("c", C_TMP)], w=[("c", C_ABS)], est=170)
        pl.op("tensor", lambda e: e.matmul(
            cx_ps[0][0:1, 0:1], col(C_ABS), ones_cf[:, :], start=True, stop=True,
            skip_group_check=True),
            r=[("c", C_ABS), "oncf"],
            w=[("ctxp", 0)], est=300, solo=True)
        pl.op("vector", lambda e: e.tensor_scalar(
            out=sc(S_BETA), in0=cx_ps[0][0:1, 0:1], scalar1=1.0 / (D * D),
            scalar2=None, op0=ALU.mult),
            r=[("ctxp", 0)], w=[("s", S_BETA)], est=170)
        pl.op("vector", lambda e: e.tensor_scalar(
            out=sc(S_INV), in0=sc(S_BETA), scalar1=EPS, scalar2=None,
            op0=ALU.add), r=[("s", S_BETA)], w=[("s", S_INV)], est=170)
        pl.op("vector", lambda e: e.tensor_scalar(
            out=sc(S_INV), in0=sc(S_INV), scalar1=0.5, scalar2=None,
            op0=ALU.mult), r=[("s", S_INV)], w=[("s", S_INV)], est=170)
        pl.op("tensor", lambda e: e.matmul(
            cx_ps[0][:, 0:1], ones_rf[:1, :], sc(S_INV), start=True, stop=True,
            skip_group_check=True),
            r=["onrf", ("s", S_INV)],
            w=[("ctxp", 0)], est=300, solo=True)
        pl.op("vector", lambda e: e.tensor_copy(
            out=col(C_INVB), in_=cx_ps[0][:, 0:1]),
            r=[("ctxp", 0)], w=[("c", C_INVB)], est=170)
        pl.op("vector", lambda e: e.tensor_scalar(
            out=col(59), in0=col(C_INVB), scalar1=-1.0, scalar2=None,
            op0=ALU.mult), r=[("c", C_INVB)], w=[("c", 59)], est=120)
        for kc in range(CH):
            sl = kc % 2
            stg = wst[:, sl * D:(sl + 1) * D]
            tl = tmp_ln[:, sl * D:(sl + 1) * D]
            ao = aosb[:, sl * D:(sl + 1) * D]
            pl.op("sync", lambda e, stg=stg, kc=kc: e.dma_start(
                out=stg, in_=wf_d[kc * P:(kc + 1) * P, :]),
                w=[("wst", sl)], dma=True, dsem=f"wst{sl}", est=2400)
            pl.op("vector", lambda e, stg=stg, ao=ao, sl=sl: e.tensor_scalar(
                out=ao, in0=stg, scalar1=col(C_INVB), scalar2=None,
                op0=ALU.is_gt), r=[("wst", sl), ("c", C_INVB)],
                w=[("ao", sl, 0), ("ao", sl, 1)], est=2000)
            pl.op("vector", lambda e, stg=stg, tl=tl, sl=sl: e.tensor_scalar(
                out=tl, in0=stg, scalar1=col(59), scalar2=None,
                op0=ALU.is_lt), r=[("wst", sl), ("c", 59)],
                w=[("tl", sl, 0), ("tl", sl, 1)], est=2000)
            pl.op("vector", lambda e, kc=kc, tl=tl, ao=ao, sl=sl: e.tensor_tensor(
                out=wa[:, kc * D:(kc + 1) * D], in0=ao, in1=tl,
                op=ALU.subtract),
                r=[("ao", sl, 0), ("ao", sl, 1), ("tl", sl, 0),
                   ("tl", sl, 1)],
                w=[("wa", kc)], est=2400)
        idw = wst[:, 0:P]
        pl.op("gpsimd", lambda e, idw=idw: e.memset(idw, 0.0),
              r=[("wst", 0)], w=[("wst", 0)], est=150)
        pl.op("gpsimd", lambda e, idw=idw: e.affine_select(
            out=idw, in_=idw, compare_op=ALU.not_equal,
            fill=1.0, base=0, pattern=[[-1, P]], channel_multiplier=1),
            r=[("wst", 0)], w=[("wst", 0)], est=400)
        pl.op("tensor", lambda e, idw=idw: e.transpose(
            cx_ps[0][0:1, :], col(C_GMAX), idw),
            r=[("c", C_GMAX), ("wst", 0)], w=[("ctxp", 0)], est=300,
            solo=True)
        pl.op("vector", lambda e: e.tensor_reduce(
            out=sc(S_GAM), in_=cx_ps[0][0:1, 0:P], axis=AX.X, op=ALU.max),
            r=[("ctxp", 0)], w=[("s", S_GAM)], est=220)
        pl.op("sync", lambda e: e.dma_start(
            out=cc_in.ap(), in_=sc(S_GAM)), r=[("s", S_GAM)], dma=True,
            dsem="ccw", est=1500)
        pl.op("gpsimd", lambda e: e.collective_compute(
            "AllReduce", ALU.max, replica_groups=[list(range(8))],
            ins=[cc_in.ap()], outs=[cc_out.ap()]),
            deps=[len(pl.ops) - 1], cc=True)
        pl.op("sync", lambda e: e.dma_start(
            out=sc(S_GAM), in_=cc_out.ap()),
            w=[("s", S_GAM)], deps=[len(pl.ops) - 1], dma=True, dsem="ccr",
            est=1500)
        pl.op("vector", lambda e: e.reciprocal(out=sc(S_S1), in_=sc(S_GAM)),
              r=[("s", S_GAM)], w=[("s", S_S1)], est=170)
        pl.op("vector", lambda e: e.tensor_scalar(
            out=sc(S_S1), in0=sc(S_S1), scalar1=QB, scalar2=None, op0=ALU.mult),
            r=[("s", S_S1)], w=[("s", S_S1)], est=170)
        pl.op("vector", lambda e: e.tensor_tensor(
            out=sc(S_S2), in0=sc(S_GAM), in1=sc(S_BETA), op=ALU.mult),
            r=[("s", S_GAM), ("s", S_BETA)], w=[("s", S_S2)], est=170)
        pl.op("vector", lambda e: e.tensor_scalar(
            out=sc(S_S2), in0=sc(S_S2), scalar1=1.0 / QB, scalar2=None,
            op0=ALU.mult), r=[("s", S_S2)], w=[("s", S_S2)], est=170)
        pl.op("tensor", lambda e: e.matmul(
            cx_ps[0][:, 0:1], ones_rf[:1, :], sc(S_S1), start=True, stop=True,
            skip_group_check=True),
            r=["onrf", ("s", S_S1)], w=[("ctxp", 0)],
            est=300, solo=True)
        pl.op("vector", lambda e: e.tensor_copy(
            out=col(C_S1B), in_=cx_ps[0][:, 0:1]),
            r=[("ctxp", 0)], w=[("c", C_S1B)], est=170)
        pl.op("tensor", lambda e: e.matmul(
            cx_ps[1][:, 0:1], ones_rf[:1, :], sc(S_S2), start=True, stop=True,
            skip_group_check=True),
            r=["onrf", ("s", S_S2)], w=[("ctxp", 1)],
            est=300, solo=True)
        pl.op("vector", lambda e: e.tensor_copy(
            out=col(C_S2B), in_=cx_ps[1][:, 0:1]),
            r=[("ctxp", 1)], w=[("c", C_S2B)], est=170)

        CLIP = QB - EPS
        toks = [None]

        def b2_clip(g):
            s4 = g % 4
            qs = qfm[:, s4 * D:(s4 + 1) * D]
            pl.op("vector", lambda e, g=g, qs=qs: e.tensor_scalar(
                out=qs, in0=v_tm[:, g * D:(g + 1) * D],
                scalar1=col(C_S1B), scalar2=CLIP, op0=ALU.mult, op1=ALU.min),
                r=[("v", g, 0), ("v", g, 1), ("c", C_S1B)], w=[("qf", s4)],
                est=1300)
            pl.op("vector", lambda e, qs=qs: e.tensor_scalar(
                out=qs, in0=qs, scalar1=-CLIP, scalar2=None, op0=ALU.max),
                r=[("qf", s4)], w=[("qf", s4)], est=1100)

        def b2_dst(g):
            s4 = g % 4
            buf = aosb if s4 < 2 else wst
            return buf[:, (s4 % 2) * D:(s4 % 2 + 1) * D], s4

        def b2_xdma(g):
            xsl = g % 2
            xa = xbuf[:, xsl * D:(xsl + 1) * D]
            pl.op("sync", lambda e, g=g, xa=xa: e.dma_start(
                out=xa, in_=xnew_d[g * P:(g + 1) * P, :]),
                r=[("xnd", g)], w=[("xb", xsl)], dma=True, dsem=f"xb{xsl}",
                est=2400)

        def b2_mm(g, nh):
            s4 = g % 4
            dst, _ = b2_dst(g)
            wkeys = [("b2o", s4, nh)] + ([("wst", s4 % 2)] if s4 >= 2 else [])
            mm_group(
                lambda m2, s4=s4: qfm[:, s4 * D + m2 * P:
                                      s4 * D + (m2 + 1) * P],
                lambda m2, nh=nh: wa[:, m2 * D + nh * 512:
                                     m2 * D + (nh + 1) * 512],
                lambda m2, s4=s4: [("wa", m2), ("qf", s4)],
                lambda e, src, nh=nh, dst=dst: e.activation(
                    out=dst[:, nh * 512:(nh + 1) * 512], in_=src,
                    func=AF.Copy, scale=col(C_S2B)),
                [("c", C_S2B)], wkeys,
                evac_eng="scalar", evac_est=700)

        def b2_ln(g):
            dst, s4 = b2_dst(g)
            ln_stats(dst, [("b2o", s4, 0), ("b2o", s4, 1)], LN_EPS_OUT, s4)
            tl = tmp_ln[:, (g % 2) * D:(g % 2 + 1) * D]
            ln_apply(tl, [("tl", g % 2, 0), ("tl", g % 2, 1)], dst,
                     [("b2o", s4, 0), ("b2o", s4, 1)], s4, eng="vector",
                     est=1300)

        def b2_out(g):
            xsl = g % 2
            tl = tmp_ln[:, xsl * D:(xsl + 1) * D]
            xa = xbuf[:, xsl * D:(xsl + 1) * D]
            pl.op("vector", lambda e, xa=xa, tl=tl: e.tensor_tensor(
                out=xa, in0=xa, in1=tl, op=ALU.add),
                r=[("xb", xsl), ("tl", xsl, 0), ("tl", xsl, 1)],
                w=[("xb", xsl)], est=2000)
            toks[0] = pl.op("gpsimd", lambda e, g=g, xa=xa: e.dma_start(
                out=y_d[g * P:(g + 1) * P, :], in_=xa),
                r=[("xb", xsl)], w=["ywd"], dma=True, dsem="yw", est=2400)

        b2_stages = [(3, b2_out), (2, b2_ln),
                     (1, lambda g: (b2_mm(g, 0), b2_mm(g, 1))),
                     (1, b2_xdma), (0, b2_clip)]
        for step in range(NT + 3):
            for lag, fn_ in b2_stages:
                g = step - lag
                if 0 <= g < NT:
                    fn_(g)
        tok = toks[0]

        pl.op("gpsimd", lambda e: e.engine_nop(), deps=[tok], est=50)

        pl.emit(nc, st)
        print(f"[kernel] planned {len(pl.ops)} ops, "
              f"est span {pl.est_span / 1e3:.1f} us")
    return nc


_NC_CACHE = None


def kernel(L0_attn_x, L0_attn_Wq, L0_attn_Wk, L0_attn_Wv, L0_attn_Wo,
           L0_ffn_weight, L0_ffn_ln_gamma, L0_ffn_ln_beta):
    global _NC_CACHE
    x = np.ascontiguousarray(np.asarray(L0_attn_x, dtype=np.float32))
    wq = np.ascontiguousarray(np.asarray(L0_attn_Wq, dtype=np.float32))
    wk = np.ascontiguousarray(np.asarray(L0_attn_Wk, dtype=np.float32))
    wv = np.ascontiguousarray(np.asarray(L0_attn_Wv, dtype=np.float32))
    wo = np.ascontiguousarray(np.asarray(L0_attn_Wo, dtype=np.float32))
    wf = np.ascontiguousarray(np.asarray(L0_ffn_weight, dtype=np.float32))
    gam = np.ascontiguousarray(np.asarray(L0_ffn_ln_gamma, dtype=np.float32)).reshape(1, D)
    bet = np.ascontiguousarray(np.asarray(L0_ffn_ln_beta, dtype=np.float32)).reshape(1, D)

    B, L, _ = x.shape
    in_maps = []
    for c in range(8):
        b, h = c // 2, c % 2
        seq = x[b]
        halo = np.zeros((P, D), np.float32) if h == 0 else \
            np.ascontiguousarray(seq[h * T - P:h * T])
        xc = np.concatenate([halo, seq[h * T:(h + 1) * T]], axis=0)
        flag = np.full((P, 1), 1.0 if h == 0 else 0.0, np.float32)
        in_maps.append({"x": xc, "wq": wq, "wk": wk, "wv": wv, "wo": wo,
                        "wf": wf, "gam": gam, "bet": bet, "flag": flag})

    if _NC_CACHE is None:
        _NC_CACHE = build_nc()
    res = run_bass_kernel_spmd(_NC_CACHE, in_maps, core_ids=list(range(8)))
    out = np.empty((B, L, D), np.float32)
    for c in range(8):
        b, h = c // 2, c % 2
        out[b, h * T:(h + 1) * T, :] = res.results[c]["y"]
    return out


# version 60 (final)
# speedup vs baseline: 1.0115x; 1.0115x over previous
import contextlib
import heapq
import numpy as np

import concourse.bass as bass
import concourse.mybir as mybir
from concourse.bass_utils import run_bass_kernel_spmd

F32 = mybir.dt.float32
BF16 = mybir.dt.bfloat16
AF = mybir.ActivationFunctionType
ALU = mybir.AluOpType
AX = mybir.AxisListType

P = 128
D = 1024
H = 16
DH = 64
T = 2048
NT = 16
NCH = 4
CH = 8
QB = 128.0
EPS = 1e-5
LN_EPS_OUT = 1e-5
LN_EPS_IN = 1e-3
NEG = -1e30
KFW = P + T

ENGINES = ("sync", "tensor", "vector", "scalar", "gpsimd")
DEF_DUR = {"sync": 150, "tensor": 260, "vector": 500, "scalar": 450,
           "gpsimd": 900}


HOP = 420.0


class Plan:
    def __init__(self):
        self.ops = []
        self.last_w = {}
        self.readers = {}
        self._gid_auto = 0

    def op(self, eng, fn, r=(), w=(), dma=False, cc=False, deps=(),
           dsem=None, est=None, gid=None, solo=False):
        d = set(deps)
        if eng == "tensor" and not dma and not cc and gid is None:
            self._gid_auto += 1
            gid = ("auto", self._gid_auto)
        for k in r:
            if k in self.last_w:
                d.add(self.last_w[k])
        for k in w:
            if k in self.last_w:
                d.add(self.last_w[k])
            for rd in self.readers.get(k, ()):
                d.add(rd)
        i = len(self.ops)
        d.discard(i)
        for k in r:
            self.readers.setdefault(k, []).append(i)
        for k in w:
            self.last_w[k] = i
            self.readers[k] = []
        self.ops.append(dict(eng=eng, fn=fn, deps=d, dma=dma, cc=cc,
                             dsem=dsem, est=est, gid=gid, solo=solo))
        return i

    def schedule(self):
        ops = self.ops
        n = len(ops)
        succ = [[] for _ in range(n)]
        indeg = [0] * n
        for i, o in enumerate(ops):
            for d in o["deps"]:
                succ[d].append(i)
                indeg[i] += 1
        from collections import Counter
        is_pe_op = [o["eng"] == "tensor" and not o["dma"] and not o["cc"]
                    for o in ops]
        is_solo = [is_pe_op[i] and bool(o.get("solo")) for i, o in
                   enumerate(ops)]
        gid_of = [o["gid"] if (is_pe_op[i] and not is_solo[i]) else None
                  for i, o in enumerate(ops)]
        gsize = Counter(g for g in gid_of if g is not None)
        ext_rem = Counter()
        for i, o in enumerate(ops):
            if gid_of[i] is None:
                continue
            for d in o["deps"]:
                if gid_of[d] != gid_of[i]:
                    ext_rem[gid_of[i]] += 1
        pe_gid = [None]
        pe_rem = dict(gsize)
        def durf(i):
            o = ops[i]
            if o["dma"]:
                return o["est"] or 2200
            if o["cc"]:
                return 26000
            return o["est"] or DEF_DUR[o["eng"]]
        prio = [0.0] * n
        for i in range(n - 1, -1, -1):
            p = 0.0
            for s in succ[i]:
                hop = 0.0 if ops[s]["eng"] == ops[i]["eng"] else HOP
                p = max(p, prio[s] + hop)
            prio[i] = p + durf(i)
        ready = {e: set() for e in ENGINES}
        for i, o in enumerate(ops):
            if indeg[i] == 0:
                ready[o["eng"]].add(i)
        free = {e: 0.0 for e in ENGINES}
        comp = [0.0] * n
        rdy = [0.0] * n
        rdy_src = [-1] * n
        expw = [0.0] * n
        order = []

        def pick(e):
            h = ready[e]
            if not h:
                return None
            if e == "tensor":
                if pe_gid[0] is not None:
                    cand = [i for i in h
                            if is_solo[i] or gid_of[i] == pe_gid[0]]
                else:
                    cand = [i for i in h
                            if is_solo[i] or ext_rem[gid_of[i]] == 0]
                if not cand:
                    return None
            else:
                cand = h
            fe = free[e]
            return min(cand, key=lambda i: (max(fe, rdy[i]), -prio[i], i))

        for _step in range(n):
            best = None
            for e in ENGINES:
                i = pick(e)
                if i is None:
                    continue
                st = max(free[e], rdy[i])
                key = (st, -prio[i], i)
                if best is None or key < best[0]:
                    best = (key, i, e)
            if best is None:
                raise RuntimeError(
                    f"scheduler wedged at step {_step}: pe_gid={pe_gid[0]}")
            (st, _, _), i, e = best
            expw[i] = max(0.0, rdy[i] - free[e])
            ready[e].discard(i)
            if e == "tensor" and not is_solo[i]:
                g = gid_of[i]
                pe_rem[g] -= 1
                pe_gid[0] = g if pe_rem[g] > 0 else None
            o = ops[i]
            if o["dma"]:
                free[e] = st + (700 if e == "gpsimd" else 600)
                comp[i] = st + (o["est"] or 2200)
            elif o["cc"]:
                free[e] = st + 300
                comp[i] = st + 26000
            else:
                dur = o["est"] or DEF_DUR[e]
                free[e] = comp[i] = st + dur
            order.append(i)
            for s2 in succ[i]:
                hop = 0.0 if ops[s2]["eng"] == e else HOP
                if comp[i] + hop > rdy[s2]:
                    rdy[s2] = comp[i] + hop
                    rdy_src[s2] = i
                indeg[s2] -= 1
                if indeg[s2] == 0:
                    ready[ops[s2]["eng"]].add(s2)
                if gid_of[s2] is not None and gid_of[s2] != gid_of[i]:
                    ext_rem[gid_of[s2]] -= 1
        self.est_span = max(comp) if n else 0.0
        self.model_comp = comp
        import os
        if os.environ.get("KN_ANALYZE"):
            from collections import Counter as C2
            agg = C2()
            aggt = C2()
            for i in order:
                if expw[i] <= 0:
                    continue
                e = ops[i]["eng"]
                se = ops[rdy_src[i]]["eng"] if rdy_src[i] >= 0 else "-"
                sd = "dma" if (rdy_src[i] >= 0 and ops[rdy_src[i]]["dma"]) \
                    else ("cc" if (rdy_src[i] >= 0 and ops[rdy_src[i]]["cc"])
                          else se)
                agg[(e, sd)] += expw[i]
                aggt[int(comp[i] / 1e5)] += expw[i]
            print("[analyze] exposed wait by (engine <- source):")
            for k, v in agg.most_common(14):
                print(f"   {k[0]:7s} <- {k[1]:7s} : {v/1e3:8.1f} us")
            print("[analyze] exposed wait per 100us bucket:",
                  {k * 100: round(v / 1e3) for k, v in sorted(aggt.items())})
            prev_on_eng = {}
            prev_of = [-1] * n
            for i in order:
                e = ops[i]["eng"]
                prev_of[i] = prev_on_eng.get(e, -1)
                prev_on_eng[e] = i
            start = [comp[i] - durf(i) for i in range(n)]
            cur = max(range(n), key=lambda i: comp[i])
            attr = C2()
            pathn = C2()
            while cur >= 0:
                o = ops[cur]
                kind = ("dma" if o["dma"] else "cc" if o["cc"] else o["eng"])
                attr[("dur", kind)] += durf(cur)
                pathn[kind] += 1
                ds = rdy_src[cur]
                dep_t = rdy[cur]
                pe = prev_of[cur]
                eng_t = comp[pe] if pe >= 0 else 0.0
                if ds >= 0 and dep_t >= eng_t and dep_t >= start[cur] - 1:
                    hop = 0.0 if ops[ds]["eng"] == o["eng"] else HOP
                    attr[("hop", f"{ops[ds]['eng']}->{o['eng']}")] += hop
                    cur = ds
                elif pe >= 0:
                    attr[("busy", o["eng"])] += max(0.0, start[cur] - eng_t)
                    cur = pe
                else:
                    break
            print("[analyze] critical path attribution (us):")
            for k, v in attr.most_common(18):
                print(f"   {k[0]:5s} {k[1]:16s} : {v/1e3:8.1f}")
            print("[analyze] path op counts:", dict(pathn))
        return order

    def emit(self, nc, stack):
        import os
        tr = os.environ.get("KN_TRUNC")
        if tr:
            self.ops = self.ops[:int(tr)]
        ops = self.ops
        n = len(ops)
        def is_pe(o):
            return o["eng"] == "tensor" and not o["dma"] and not o["cc"]
        hard = [None] * n
        for i, o in enumerate(ops):
            if is_pe(o):
                hard[i] = {d for d in o["deps"] if not is_pe(ops[d])}
            else:
                hard[i] = o["deps"]
        import os
        if os.environ.get("KN_NOSCHED"):
            self.schedule()
            order = list(range(n))
        else:
            order = self.schedule()
        needed = set()
        for i in range(n):
            needed |= hard[i]
        sem_keys = set()
        for i, o in enumerate(ops):
            if o["dma"]:
                sem_keys.add(("d", o["dsem"]))
            elif o["cc"]:
                sem_keys.add(("c", 0))
            elif i in needed:
                sem_keys.add(("e", o["eng"]))
        sems = {k: stack.enter_context(nc.semaphore(f"s_{k[0]}_{k[1]}"))
                for k in sorted(sem_keys)}
        val = [None] * n
        cnt = {}
        for i in order:
            o = ops[i]
            if o["dma"]:
                k = ("d", o["dsem"])
                cnt[k] = cnt.get(k, 0) + 16
                val[i] = (k, cnt[k])
            elif o["cc"]:
                k = ("c", 0)
                cnt[k] = cnt.get(k, 0) + 1
                val[i] = (k, cnt[k])
            elif i in needed:
                k = ("e", o["eng"])
                cnt[k] = cnt.get(k, 0) + 1
                val[i] = (k, cnt[k])
        streams = {e: [] for e in ENGINES}
        wm = {e: {} for e in ENGINES}
        cover = [None] * n
        for i in order:
            o = ops[i]
            e = o["eng"]
            for dd in sorted(hard[i]):
                s, v = val[dd]
                if wm[e].get(s, 0) >= v:
                    continue
                streams[e].append(("wait", s, v))
                wm[e][s] = v
                for s2, v2 in cover[dd].items():
                    if wm[e].get(s2, 0) < v2:
                        wm[e][s2] = v2
            streams[e].append(("op", i))
            cover[i] = dict(wm[e])
            if val[i] is not None:
                s, v = val[i]
                cover[i][s] = max(cover[i].get(s, 0), v)

        def body_for(items):
            def body(eng):
                for it in items:
                    if it[0] == "wait":
                        eng.wait_ge(sems[it[1]], it[2])
                    else:
                        o = ops[it[1]]
                        ins = o["fn"](eng)
                        if val[it[1]] is not None:
                            s, v = val[it[1]]
                            ins.then_inc(sems[s], 16 if o["dma"] else 1)
            return body

        with nc.Block() as block:
            for ename in ENGINES:
                getattr(block, ename)(body_for(streams[ename]))


def build_nc():
    nc = bass.Bass("TRN2", target_bir_lowering=False)

    x_d = nc.dram_tensor("x", [P + T, D], F32, kind="ExternalInput").ap()
    wq_d = nc.dram_tensor("wq", [D, D], F32, kind="ExternalInput").ap()
    wk_d = nc.dram_tensor("wk", [D, D], F32, kind="ExternalInput").ap()
    wv_d = nc.dram_tensor("wv", [D, D], F32, kind="ExternalInput").ap()
    wo_d = nc.dram_tensor("wo", [D, D], F32, kind="ExternalInput").ap()
    wf_d = nc.dram_tensor("wf", [D, D], F32, kind="ExternalInput").ap()
    gam_d = nc.dram_tensor("gam", [1, D], F32, kind="ExternalInput").ap()
    bet_d = nc.dram_tensor("bet", [1, D], F32, kind="ExternalInput").ap()
    flag_d = nc.dram_tensor("flag", [P, 1], F32, kind="ExternalInput").ap()
    y_d = nc.dram_tensor("y", [T, D], F32, kind="ExternalOutput").ap()
    xnew_d = nc.dram_tensor("xnew", [T, D], F32).ap()
    gmx_d = nc.dram_tensor("gmx", [P, 1], F32)
    cc_in = nc.dram_tensor("cc_in", [1, 1], F32)
    cc_out = nc.dram_tensor("cc_out", [1, 1], F32)
    ccw_in = nc.dram_tensor("ccw_in", [1, 1], F32)
    ccw_out = nc.dram_tensor("ccw_out", [1, 1], F32)

    pl = Plan()

    with contextlib.ExitStack() as st:
        sb = lambda name, shape, dt: st.enter_context(nc.sbuf_tensor(name, shape, dt))
        ps = lambda name, shape, dt: st.enter_context(nc.psum_tensor(name, shape, dt))

        xbuf = sb("xbuf", [P, 2 * D], F32)
        wa = sb("wa", [P, CH * D], BF16)
        wb = sb("wb", [P, CH * D], BF16)
        wc = sb("wc", [P, CH * D], BF16)
        k_fm = sb("k_fm", [P, CH * KFW], BF16)
        v_tm = sb("v_tm", [P, 17 * D], BF16)
        lnfm = sb("lnfm", [P, CH * KFW], BF16)
        qfm = sb("qfm", [P, CH * 512], BF16)
        wst = sb("wst", [P, 2 * D], F32)
        ln0bf = sb("ln0bf", [P, D], BF16)
        tmp_ln = sb("tmp_ln", [P, 2 * D], F32)
        aosb = sb("aosb", [P, 2 * D], F32)
        pn = sb("pn", [P, 4 * P], BF16)
        pta = sb("pta", [P, 4 * P], BF16)
        ctxfm = sb("ctxfm", [P, CH * P], BF16)
        gbg = sb("gbg", [P, D], F32)
        gbb = sb("gbb", [P, D], F32)
        ident = sb("ident", [P, P], BF16)
        mask_r = sb("mask_r", [P, P], BF16)
        mask_0 = sb("mask_0", [P, P], BF16)
        mask_l = sb("mask_l", [P, P], BF16)
        mex = wst[:, 0:P]
        flag_sb = sb("flag_sb", [P, 1], F32)
        ones_rf = sb("ones_rf", [1, P], F32)
        ones_cf = sb("ones_cf", [P, 1], F32)
        kconst = sb("kconst", [P, 8], F32)
        cols = sb("cols", [P, 88], F32)
        bnst = sb("bnst", [P, 48], F32)
        scs = sb("scs", [1, 16], F32)

        s_ps = [ps(f"s_ps{i}", [P, 512], F32) for i in range(2)]
        tp_ps = [ps(f"tp_ps{i}", [P, P], BF16) for i in range(2)]
        cx_ps = [ps(f"cx_ps{i}", [P, P], F32) for i in range(2)]
        mm_ps = [ps(f"mm_ps{i}", [P, 512], F32) for i in range(2)]

        col = lambda i: cols[:, i:i + 1]
        ST_MEAN, ST_VAR, ST_SD, ST_RSTD, ST_NMR = range(5)
        stc = lambda par, i: cols[:, 8 * par + i:8 * par + i + 1]
        atc = lambda s, i: cols[:, 32 + 4 * s + i:32 + 4 * s + i + 1]
        C_GMAX, C_ABS, C_INVB, C_S1B, C_S2B, C_TMP, C_TMP2 = 80, 81, 82, 83, 84, 85, 86
        C_TMP3 = 79
        S_BETA, S_INV, S_GAM, S_S1, S_S2 = range(5)
        sc = lambda i: scs[:, i:i + 1]

        KEPS5, KEPS3, KNEG1, KM05 = 0, 1, 2, 3
        EPS_SLOT = {LN_EPS_OUT: KEPS5, LN_EPS_IN: KEPS3}
        kc_ap = lambda i: kconst[:, i:i + 1]

        tctr = [0]
        mctr = [0]

        pl.op("sync", lambda e: e.dma_start(out=flag_sb[:, :], in_=flag_d),
              w=["flag"], dma=True, dsem="flag", est=800)
        pl.op("gpsimd", lambda e: e.memset(ident[:, :], 0.0), w=["id"], est=150)
        pl.op("gpsimd", lambda e: e.affine_select(
            out=ident[:, :], in_=ident[:, :], compare_op=ALU.not_equal,
            fill=1.0, base=0, pattern=[[-1, P]], channel_multiplier=1),
            r=["id"], w=["id"], est=400)
        pl.op("gpsimd", lambda e: e.memset(mask_r[:, :], 1.0), w=["mr"], est=200)
        pl.op("gpsimd", lambda e: e.affine_select(
            out=mask_r[0:64, :], in_=mask_r[0:64, :], compare_op=ALU.is_ge,
            fill=0.0, base=0, pattern=[[1, P]], channel_multiplier=-1),
            r=["mr"], w=["mr"], est=400)
        pl.op("gpsimd", lambda e: e.affine_select(
            out=mask_r[0:64, :], in_=mask_r[0:64, :], compare_op=ALU.is_ge,
            fill=0.0, base=63, pattern=[[-1, P]], channel_multiplier=1),
            r=["mr"], w=["mr"], est=400)
        pl.op("sync", lambda e: e.dma_start(
            out=mask_r[64:128, :], in_=mask_r[0:64, :]),
            r=["mr"], w=["mr"], dma=True, dsem="mcp", est=800)
        for i, v in enumerate([1e-5, 1e-3, -1.0, -0.5, 0.125]):
            pl.op("gpsimd", lambda e, i=i, v=v: e.memset(kconst[:, i:i + 1], v),
                  w=[("kc", i)], est=100)
        pl.op("vector", lambda e: e.tensor_scalar(
            out=kconst[:, 5:6], in0=flag_sb[:, 0:1], scalar1=-1.0,
            scalar2=1.0, op0=ALU.mult, op1=ALU.add),
            r=["flag"], w=[("kc", 5)], est=120)
        pl.op("gpsimd", lambda e: e.memset(mex[:, :], 1.0), w=[("wst", 0)], est=200)
        pl.op("gpsimd", lambda e: e.affine_select(
            out=mex[:, :], in_=mex[:, :], compare_op=ALU.is_ge,
            fill=0.0, base=-63, pattern=[[1, P]], channel_multiplier=0),
            r=[("wst", 0)], w=[("wst", 0)], est=400)
        pl.op("vector", lambda e: e.tensor_scalar(
            out=mex[:, :], in0=mex[:, :], scalar1=kc_ap(5), scalar2=1.0,
            op0=ALU.add, op1=ALU.min),
            r=[("wst", 0), ("kc", 5)], w=[("wst", 0)], est=300)
        pl.op("vector", lambda e: e.tensor_tensor(
            out=mask_0[:, :], in0=mex[:, :], in1=mask_r[:, :], op=ALU.mult),
            r=[("wst", 0), "mr"], w=["m0"], est=350)
        pl.op("vector", lambda e: e.tensor_copy(
            out=mask_l[:, :], in_=mask_r[:, :]), r=["mr"], w=["ml"], est=250)
        pl.op("gpsimd", lambda e: e.memset(mask_l[:, 127:128], 0.0),
              r=["ml"], w=["ml"], est=100)
        pl.op("gpsimd", lambda e: e.memset(ones_rf[:1, :], 1.0), w=["onrf"], est=100)
        pl.op("vector", lambda e: e.memset(col(C_GMAX), 0.0),
              w=[("c", C_GMAX)], est=100)
        pl.op("gpsimd", lambda e: e.memset(ones_cf[:, :], 1.0), w=["oncf"], est=100)
        pl.op("gpsimd", lambda e: e.memset(v_tm[:, 16 * D:17 * D], 0.0),
              w=[("v", 16, 0), ("v", 16, 1)], est=600)
        pl.op("gpsimd", lambda e: e.dma_start(
            out=ccw_in.ap(), in_=ones_rf[0:1, 0:1]),
            r=["onrf"], dma=True, dsem="ccww", est=1200)
        pl.op("gpsimd", lambda e: e.collective_compute(
            "AllReduce", ALU.max, replica_groups=[list(range(8))],
            ins=[ccw_in.ap()], outs=[ccw_out.ap()]),
            deps=[len(pl.ops) - 1], cc=True)

        wctr = [0]

        def load_weight_bf16(wdram, dst, dst_key):
            for kc in range(CH):
                s4 = wctr[0] % 4
                wctr[0] += 1
                if s4 < 2:
                    stg = wst[:, s4 * D:(s4 + 1) * D]
                    skeys = [("wst", s4)]
                else:
                    stg = tmp_ln[:, (s4 - 2) * D:(s4 - 1) * D]
                    skeys = [("tl", s4 - 2, 0), ("tl", s4 - 2, 1)]
                pl.op("sync", lambda e, stg=stg, kc=kc, wdram=wdram: e.dma_start(
                    out=stg, in_=wdram[kc * P:(kc + 1) * P, :]),
                    w=skeys, dma=True, dsem=f"wst{s4}", est=2400)
                pl.op("scalar", lambda e, stg=stg, kc=kc: e.copy(
                    out=dst[:, kc * D:(kc + 1) * D], in_=stg),
                    r=skeys, w=[(dst_key, kc)], est=1100)

        def ln_stats(src_ap, src_keys, eps, par, eng2="vector"):
            b0 = 12 * par
            pl.op("vector", lambda e: e.bn_stats(
                out=bnst[:, b0:b0 + 6], in_=src_ap[:, 0:512]),
                r=src_keys, w=[("bn", par)], est=620)
            pl.op(eng2, lambda e: e.bn_stats(
                out=bnst[:, b0 + 6:b0 + 12], in_=src_ap[:, 512:1024]),
                r=src_keys, w=[("bn2", par)],
                est=620 if eng2 == "vector" else 1000)
            pl.op("vector", lambda e: e.bn_aggr(
                out=cols[:, 8 * par:8 * par + 2], in_=bnst[:, b0:b0 + 12]),
                r=[("bn", par), ("bn2", par)],
                w=[("st", par, ST_MEAN), ("st", par, ST_VAR)], est=150)
            es = EPS_SLOT[eps]
            pl.op("scalar", lambda e, es=es, par=par: e.activation(
                out=stc(par, ST_SD), in_=stc(par, ST_VAR), func=AF.Ln,
                bias=kc_ap(es), scale=1.0),
                r=[("st", par, ST_VAR), ("kc", es)], w=[("st", par, ST_SD)],
                est=250)
            pl.op("scalar", lambda e, par=par: e.activation(
                out=stc(par, ST_RSTD), in_=stc(par, ST_SD), func=AF.Exp,
                bias=0.0, scale=kc_ap(KM05)),
                r=[("st", par, ST_SD), ("kc", KM05)],
                w=[("st", par, ST_RSTD)], est=250)
            pl.op("vector", lambda e, par=par: e.tensor_scalar(
                out=stc(par, ST_NMR), in0=stc(par, ST_MEAN),
                scalar1=stc(par, ST_RSTD), scalar2=-1.0,
                op0=ALU.mult, op1=ALU.mult),
                r=[("st", par, ST_MEAN), ("st", par, ST_RSTD)],
                w=[("st", par, ST_NMR)], est=170)

        def ln_apply(dst_ap, dst_keys, src_ap, src_keys, par, eng="gpsimd",
                     est=1250):
            pl.op(eng, lambda e: e.tensor_scalar(
                out=dst_ap, in0=src_ap, scalar1=stc(par, ST_RSTD),
                scalar2=stc(par, ST_NMR), op0=ALU.mult, op1=ALU.add),
                r=list(src_keys) + [("st", par, ST_RSTD), ("st", par, ST_NMR)],
                w=list(dst_keys), est=est)

        HD = D // 2

        def ln_stats_sc(src_ap, src_keys, eps, par):
            b = par % 2
            c0 = 60 + 4 * par
            for h in range(2):
                pl.op("scalar", lambda e, h=h: e.activation(
                    out=s_ps[b][:, 0:512], in_=src_ap[:, h * 512:(h + 1) * 512],
                    func=AF.Copy, bias=0.0, scale=1.0, accum_out=col(c0 + h)),
                    r=list(src_keys), w=[("sp", b), ("c", c0 + h)], est=740)
                pl.op("scalar", lambda e, h=h: e.activation(
                    out=s_ps[b][:, 0:512], in_=src_ap[:, h * 512:(h + 1) * 512],
                    func=AF.Square, bias=0.0, scale=1.0,
                    accum_out=col(c0 + 2 + h)),
                    r=list(src_keys), w=[("sp", b), ("c", c0 + 2 + h)], est=740)
            pl.op("vector", lambda e: e.tensor_tensor(
                out=stc(par, ST_MEAN), in0=col(c0), in1=col(c0 + 1),
                op=ALU.add), r=[("c", c0), ("c", c0 + 1)],
                w=[("st", par, ST_MEAN)], est=120)
            pl.op("vector", lambda e: e.tensor_scalar(
                out=stc(par, ST_MEAN), in0=stc(par, ST_MEAN),
                scalar1=1.0 / D, scalar2=None, op0=ALU.mult),
                r=[("st", par, ST_MEAN)], w=[("st", par, ST_MEAN)], est=120)
            pl.op("vector", lambda e: e.tensor_tensor(
                out=stc(par, ST_VAR), in0=col(c0 + 2), in1=col(c0 + 3),
                op=ALU.add), r=[("c", c0 + 2), ("c", c0 + 3)],
                w=[("st", par, ST_VAR)], est=120)
            pl.op("vector", lambda e: e.tensor_scalar(
                out=col(c0), in0=stc(par, ST_MEAN),
                scalar1=stc(par, ST_MEAN), scalar2=None, op0=ALU.mult),
                r=[("st", par, ST_MEAN)], w=[("c", c0)], est=120)
            pl.op("vector", lambda e: e.scalar_tensor_tensor(
                out=stc(par, ST_VAR), in0=stc(par, ST_VAR), scalar=1.0 / D,
                in1=col(c0), op0=ALU.mult, op1=ALU.subtract),
                r=[("st", par, ST_VAR), ("c", c0)], w=[("st", par, ST_VAR)],
                est=120)
            es = EPS_SLOT[eps]
            pl.op("scalar", lambda e, es=es, par=par: e.activation(
                out=stc(par, ST_SD), in_=stc(par, ST_VAR), func=AF.Ln,
                bias=kc_ap(es), scale=1.0),
                r=[("st", par, ST_VAR), ("kc", es)], w=[("st", par, ST_SD)],
                est=250)
            pl.op("scalar", lambda e, par=par: e.activation(
                out=stc(par, ST_RSTD), in_=stc(par, ST_SD), func=AF.Exp,
                bias=0.0, scale=kc_ap(KM05)),
                r=[("st", par, ST_SD), ("kc", KM05)],
                w=[("st", par, ST_RSTD)], est=250)
            pl.op("vector", lambda e, par=par: e.tensor_scalar(
                out=stc(par, ST_NMR), in0=stc(par, ST_MEAN),
                scalar1=stc(par, ST_RSTD), scalar2=-1.0,
                op0=ALU.mult, op1=ALU.mult),
                r=[("st", par, ST_MEAN), ("st", par, ST_RSTD)],
                w=[("st", par, ST_NMR)], est=170)

        def ln_apply_sp(dst_ap, dkeys_of, src_ap, skeys_of, par):
            for h, eng, es in ((0, "vector", 700), (1, "gpsimd", 1300)):
                c0, c1 = h * HD, (h + 1) * HD
                pl.op(eng, lambda e, c0=c0, c1=c1: e.tensor_scalar(
                    out=dst_ap[:, c0:c1], in0=src_ap[:, c0:c1],
                    scalar1=stc(par, ST_RSTD), scalar2=stc(par, ST_NMR),
                    op0=ALU.mult, op1=ALU.add),
                    r=list(skeys_of(h)) + [("st", par, ST_RSTD),
                                           ("st", par, ST_NMR)],
                    w=list(dkeys_of(h)), est=es)

        def transpose128(src_ap, src_keys, dst_ap, dst_keys, width=P,
                         evac_eng="vector", bank=None):
            if bank is None:
                s = tctr[0] % 2
                tctr[0] += 1
            else:
                s = bank
            pl.op("tensor", lambda e, s=s: e.transpose(
                tp_ps[s][0:width, :], src_ap, ident[:, :]),
                r=list(src_keys) + ["id"], w=[("tp", s)], est=230,
                solo=True)
            if evac_eng == "vector":
                pl.op("vector", lambda e, s=s: e.tensor_copy(
                    out=dst_ap, in_=tp_ps[s][0:width, :]),
                    r=[("tp", s)], w=list(dst_keys), est=200)
            else:
                pl.op("scalar", lambda e, s=s: e.copy(
                    out=dst_ap, in_=tp_ps[s][0:width, :]),
                    r=[("tp", s)], w=list(dst_keys), est=250)

        def mm_group(lhsT_of_kc, rhs_of_kc, rkeys_of_kc, evac_fn, evac_rextra,
                     evac_w, n_k=CH, evac_eng="scalar", evac_est=600,
                     mm_est=310, out_rows=P, out_cols=512):
            sl = mctr[0] % 2
            g = ("mm", mctr[0])
            mctr[0] += 1
            for kc in range(n_k):
                pl.op("tensor", lambda e, kc=kc, sl=sl: e.matmul(
                    mm_ps[sl][0:out_rows, 0:out_cols], lhsT_of_kc(kc),
                    rhs_of_kc(kc),
                    start=(kc == 0), stop=(kc == n_k - 1),
                    skip_group_check=True),
                    r=rkeys_of_kc(kc), w=[("mm", sl)], est=mm_est, gid=g)
            pl.op(evac_eng, lambda e, sl=sl: evac_fn(e, mm_ps[sl][0:out_rows, :]),
                  r=[("mm", sl)] + list(evac_rextra), w=list(evac_w),
                  est=evac_est)

        def ln0_tile(row0, xslot, par):
            if xslot < 2:
                xa = xbuf[:, xslot * D:(xslot + 1) * D]
                xkeys = [("xb", xslot)]
            else:
                xa = aosb[:, (xslot - 2) * D:(xslot - 1) * D]
                xkeys = [("ao", xslot - 2, 0), ("ao", xslot - 2, 1)]
            pl.op("sync", lambda e, xa=xa, row0=row0: e.dma_start(
                out=xa, in_=x_d[row0:row0 + P, :]),
                w=xkeys, dma=True, dsem=f"xb{xslot}", est=2400)
            ln_stats(xa, xkeys, LN_EPS_OUT, par)
            if xslot % 2 == 0:
                la = ln0bf[:, 0:D]
                lk_of = lambda h: [("l0", h)]
            else:
                la = ctxfm[:, 0:D]
                lk_of = lambda h: [("cfm", m, j)
                                   for m in range(4 * h, 4 * h + 4)
                                   for j in range(2)]
            ln_apply_sp(la, lk_of, xa, lambda h: xkeys, par)
            return la, lk_of

        grow = wst[0:1, 0:D]
        brow = wst[0:1, D:2 * D]
        pl.op("sync", lambda e: e.dma_start(out=grow, in_=gam_d),
              w=[("wst", 0)], dma=True, dsem="grow", est=800)
        pl.op("sync", lambda e: e.dma_start(out=brow, in_=bet_d),
              w=[("wst", 1)], dma=True, dsem="brow", est=800)
        for nh in range(2):
            pl.op("tensor", lambda e, nh=nh: e.matmul(
                mm_ps[0][:, :], ones_rf[:1, :], grow[:, nh * 512:(nh + 1) * 512],
                start=True, stop=True, skip_group_check=True),
                r=["onrf", ("wst", 0)], w=[("mm", 0)], est=500,
                solo=True)
            pl.op("vector", lambda e, nh=nh: e.tensor_copy(
                out=gbg[:, nh * 512:(nh + 1) * 512], in_=mm_ps[0][:, :]),
                r=[("mm", 0)], w=["gbg"], est=650)
            pl.op("tensor", lambda e, nh=nh: e.matmul(
                mm_ps[1][:, :], ones_rf[:1, :], brow[:, nh * 512:(nh + 1) * 512],
                start=True, stop=True, skip_group_check=True),
                r=["onrf", ("wst", 1)], w=[("mm", 1)], est=500,
                solo=True)
            pl.op("vector", lambda e, nh=nh: e.tensor_copy(
                out=gbb[:, nh * 512:(nh + 1) * 512], in_=mm_ps[1][:, :]),
                r=[("mm", 1)], w=["gbb"], est=650)


        load_weight_bf16(wk_d, wa, "wa")
        load_weight_bf16(wv_d, wb, "wb")

        def lnfm_tile(gt):
            par = (gt + 2) % 4
            la, lk_of = ln0_tile((gt + 1) * P, (gt + 1) % 4, par)
            base = (gt + 1) * P
            for kc in range(CH):
                transpose128(la[:, kc * P:(kc + 1) * P], lk_of(kc // 4),
                             lnfm[:, kc * KFW + base:kc * KFW + base + P],
                             [("lf", kc, gt)])

        lnfm_tile(-1)
        for m in range(CH):
            mm_group(
                lambda kc, m=m: wa[:, kc * D + m * P:kc * D + (m + 1) * P],
                lambda kc: lnfm[:, kc * KFW:kc * KFW + P],
                lambda kc, m=m: [("wa", kc), ("lf", kc, -1)],
                lambda e, src, m=m: e.copy(
                    out=k_fm[:, m * KFW:m * KFW + P], in_=src[:, 0:P]),
                [], [("k", m, -1)], evac_est=300, mm_est=110, out_cols=P)

        for c in range(NCH):
            for tq in range(4):
                lnfm_tile(4 * c + tq)
            for m in range(CH):
                mm_group(
                    lambda kc, m=m: wa[:, kc * D + m * P:kc * D + (m + 1) * P],
                    lambda kc, c=c: lnfm[:, kc * KFW + 512 * c + P:
                                         kc * KFW + 512 * c + 640],
                    lambda kc, c=c: [("wa", kc)] + [("lf", kc, t)
                                                    for t in range(4 * c, 4 * c + 4)],
                    lambda e, src, m=m, c=c: e.copy(
                        out=k_fm[:, m * KFW + 512 * c + P:
                                 m * KFW + 512 * c + 640], in_=src),
                    [], [("k", m, c)], evac_est=600)
            us = list(range(4 * c, 4 * c + 4)) + ([16] if c == 3 else [])
            for u in us:
                wdt = 63 if u == 16 else P
                for nh in range(2):
                    mm_group(
                        lambda kc, u=u, wdt=wdt: lnfm[:, kc * KFW + u * P + 65:
                                                      kc * KFW + u * P + 65 + wdt],
                        lambda kc, nh=nh: wb[:, kc * D + nh * 512:
                                             kc * D + (nh + 1) * 512],
                        lambda kc, u=u: [("wb", kc), ("lf", kc, u - 1),
                                         ("lf", kc, min(u, NT - 1))],
                        lambda e, src, u=u, nh=nh, wdt=wdt: e.copy(
                            out=v_tm[0:wdt, u * D + nh * 512:
                                     u * D + (nh + 1) * 512], in_=src),
                        [], [("v", u, nh)], out_rows=wdt, evac_est=600)

        load_weight_bf16(wq_d, wc, "wc")
        load_weight_bf16(wo_d, wb, "wb")

        for c in range(NCH):
            for m in range(CH):
                mm_group(
                    lambda kc, m=m: wc[:, kc * D + m * P:kc * D + (m + 1) * P],
                    lambda kc, c=c: lnfm[:, kc * KFW + 512 * c + P:
                                         kc * KFW + 512 * c + 640],
                    lambda kc, c=c: [("wc", kc)] + [("lf", kc, t)
                                                    for t in range(4 * c, 4 * c + 4)],
                    lambda e, src, m=m: e.activation(
                        out=qfm[:, m * 512:(m + 1) * 512], in_=src,
                        func=AF.Copy, scale=kc_ap(4)),
                    [("kc", 4)], [("q", m)], evac_est=600)

            def emit_tile_tail(c, g):
                par = g % 2
                for nh in range(2):
                    mm_group(
                        lambda m2: ctxfm[:, m2 * P:(m2 + 1) * P],
                        lambda m2, nh=nh: wb[:, m2 * D + nh * 512:
                                             m2 * D + (nh + 1) * 512],
                        lambda m2: [("wb", m2), ("cfm", m2, 0), ("cfm", m2, 1)],
                        lambda e, src, nh=nh, par=par: e.copy(
                            out=aosb[:, par * D + nh * 512:
                                     par * D + (nh + 1) * 512], in_=src),
                        [], [("ao", par, nh)], evac_est=600)
                ao = aosb[:, par * D:(par + 1) * D]
                last = g == NT - 1
                tl = tmp_ln[:, par * D:(par + 1) * D]
                tlk = lambda h: [("tl", par, h)]
                tlk2 = [("tl", par, 0), ("tl", par, 1)]
                aok = lambda h: [("ao", par, h)]

                def big(build, rk_of, wk_of, eng="gpsimd", est=2400):
                    if not last:
                        pl.op(eng, lambda e: build(e, 0, D),
                              r=rk_of(0) + rk_of(1), w=wk_of(0) + wk_of(1),
                              est=est)
                    else:
                        for h, e2, es in ((0, "vector", est * 45 // 100),
                                          (1, "gpsimd", est * 55 // 100)):
                            pl.op(e2, lambda e, h=h: build(e, h * HD,
                                                           (h + 1) * HD),
                                  r=rk_of(h), w=wk_of(h), est=es)

                lnst = ln_stats_sc if last else ln_stats
                ln_stats(ao, [("ao", par, 0), ("ao", par, 1)], LN_EPS_OUT,
                         par)
                big(lambda e, c0, c1: e.tensor_scalar(
                    out=tl[:, c0:c1], in0=ao[:, c0:c1],
                    scalar1=stc(par, ST_RSTD), scalar2=stc(par, ST_NMR),
                    op0=ALU.mult, op1=ALU.add),
                    lambda h: aok(h) + [("st", par, ST_RSTD),
                                        ("st", par, ST_NMR)],
                    tlk, est=1400)
                xsl = g % 2
                xa = xbuf[:, xsl * D:(xsl + 1) * D]
                pl.op("sync", lambda e, g=g, xa=xa: e.dma_start(
                    out=xa, in_=x_d[(g + 1) * P:(g + 2) * P, :]),
                    w=[("xb", xsl)], dma=True, dsem=f"xb{xsl}", est=2400)
                pl.op("vector" if last else "gpsimd",
                      lambda e, xa=xa, tl=tl, g=g: e.tensor_tensor(
                          out=xa, in0=xa, in1=tl, op=ALU.add),
                      r=[("xb", xsl)] + tlk2, w=[("xb", xsl)],
                      est=1300 if last else 2400)
                pl.op("gpsimd", lambda e, g=g, xa=xa: e.dma_start(
                    out=xnew_d[g * P:(g + 1) * P, :], in_=xa),
                    r=[("xb", xsl)], w=[("xnd", g), "xnwseq"], dma=True,
                    dsem="xnw", est=2400)
                bpar = 2 + par
                lnst(xa, [("xb", xsl)], LN_EPS_OUT, bpar)
                big(lambda e, c0, c1: e.tensor_scalar(
                    out=tl[:, c0:c1], in0=xa[:, c0:c1],
                    scalar1=stc(bpar, ST_RSTD), scalar2=stc(bpar, ST_NMR),
                    op0=ALU.mult, op1=ALU.add),
                    lambda h: [("xb", xsl), ("st", bpar, ST_RSTD),
                               ("st", bpar, ST_NMR)],
                    tlk, est=1400)
                lnst(tl, tlk2, LN_EPS_IN, bpar)
                ao2 = aosb[:, par * D:(par + 1) * D]
                big(lambda e, c0, c1: e.tensor_scalar(
                    out=ao2[:, c0:c1], in0=tl[:, c0:c1],
                    scalar1=stc(bpar, ST_RSTD), scalar2=stc(bpar, ST_NMR),
                    op0=ALU.mult, op1=ALU.add),
                    lambda h: tlk(h) + [("st", bpar, ST_RSTD),
                                        ("st", bpar, ST_NMR)],
                    aok, est=1400)
                big(lambda e, c0, c1: e.tensor_tensor(
                    out=tl[:, c0:c1], in0=ao2[:, c0:c1], in1=gbg[:, c0:c1],
                    op=ALU.mult),
                    lambda h: aok(h) + ["gbg"], tlk, est=2400)
                big(lambda e, c0, c1: e.tensor_tensor(
                    out=ln0bf[:, c0:c1], in0=tl[:, c0:c1], in1=gbb[:, c0:c1],
                    op=ALU.add),
                    lambda h: tlk(h) + ["gbb"], lambda h: [("l0", h)],
                    est=2400)
                if not last:
                    pl.op("vector", lambda e: e.tensor_reduce(
                        out=col(C_TMP2), in_=ln0bf[:, 0:D], axis=AX.X,
                        op=ALU.max, apply_absolute_value=True),
                        r=[("l0", 0), ("l0", 1)], w=[("c", C_TMP2)], est=1200)
                    pl.op("vector", lambda e: e.tensor_tensor(
                        out=col(C_GMAX), in0=col(C_GMAX), in1=col(C_TMP2),
                        op=ALU.max),
                        r=[("c", C_GMAX), ("c", C_TMP2)], w=[("c", C_GMAX)],
                        est=170)
                else:
                    for h, cdst in ((0, C_TMP2), (1, C_TMP3)):
                        pl.op("vector", lambda e, h=h, cdst=cdst:
                              e.tensor_reduce(
                                  out=col(cdst),
                                  in_=ln0bf[:, h * HD:(h + 1) * HD],
                                  axis=AX.X, op=ALU.max,
                                  apply_absolute_value=True),
                              r=[("l0", h)], w=[("c", cdst)], est=650)
                    for csrc in (C_TMP2, C_TMP3):
                        pl.op("vector", lambda e, csrc=csrc: e.tensor_tensor(
                            out=col(C_GMAX), in0=col(C_GMAX), in1=col(csrc),
                            op=ALU.max),
                            r=[("c", C_GMAX), ("c", csrc)], w=[("c", C_GMAX)],
                            est=170)
                for kc in range(CH):
                    transpose128(ln0bf[:, kc * P:(kc + 1) * P],
                                 [("l0", kc // 4)],
                                 v_tm[:, g * D + kc * P:g * D + (kc + 1) * P],
                                 [("v", g, kc // 4)], evac_eng="scalar",
                                 bank=1)

            units = [(2 * (4 * c + tq) + uu, pm)
                     for tq in range(4) for uu in range(2) for pm in range(CH)]

            def st_scores(t2, pm, uid):
                sl = uid % 2
                sgid = ("sc", t2, pm)
                spt = s_ps[sl]
                qc0 = pm * 512 + (t2 - 8 * c) * 64
                kc0 = pm * KFW + t2 * 64 + 65
                kw = 127 if t2 == 31 else P
                for hp in range(2):
                    pl.op("tensor", lambda e, hp=hp, pm=pm, spt=spt,
                          qc0=qc0, kc0=kc0, kw=kw: e.matmul(
                              spt[hp * 64:hp * 64 + 64, 0:kw],
                              qfm[hp * 64:hp * 64 + 64, qc0:qc0 + 64],
                              k_fm[hp * 64:hp * 64 + 64, kc0:kc0 + kw],
                              start=True, stop=True,
                              skip_group_check=True),
                          r=[("q", pm)] + [("k", pm, cc2) for cc2 in
                                           range(-1, t2 // 8 + 1)],
                          w=[("sp", sl)], est=240, gid=sgid, solo=True)

            def st_soft(t2, pm, uid):
                sl = uid % 2
                s4 = uid % 4
                spt = s_ps[sl]
                if t2 == 0:
                    mask_ap, mkey = mask_0, "m0"
                elif t2 == 31:
                    mask_ap, mkey = mask_l, "ml"
                else:
                    mask_ap, mkey = mask_r, "mr"
                pl.op("scalar", lambda e, s4=s4, spt=spt: e.activation(
                    out=pn[:, s4 * P:(s4 + 1) * P], in_=spt[:, 0:P],
                    func=AF.Exp, bias=0.0, scale=1.0),
                    r=[("sp", sl)], w=[("pn", s4)], est=420)
                pl.op("vector", lambda e, s4=s4, mask_ap=mask_ap:
                      e.scalar_tensor_tensor(
                          out=pn[:, s4 * P:(s4 + 1) * P],
                          in0=pn[:, s4 * P:(s4 + 1) * P], scalar=1.0,
                          in1=mask_ap[:, :], op0=ALU.mult, op1=ALU.mult,
                          accum_out=atc(s4, 1)),
                      r=[("pn", s4), mkey], w=[("pn", s4), ("at", s4, 1)],
                      est=320)

            def st_norm(t2, pm, uid):
                s4 = uid % 4
                pl.op("vector", lambda e, s4=s4: e.reciprocal(
                    out=atc(s4, 2), in_=atc(s4, 1)),
                    r=[("at", s4, 1)], w=[("at", s4, 2)], est=170)
                pl.op("vector", lambda e, s4=s4: e.tensor_scalar(
                    out=pn[:, s4 * P:(s4 + 1) * P],
                    in0=pn[:, s4 * P:(s4 + 1) * P],
                    scalar1=atc(s4, 2), scalar2=None, op0=ALU.mult),
                    r=[("pn", s4), ("at", s4, 2)], w=[("pn", s4)],
                    est=180)

            def st_tr(t2, pm, uid):
                s4 = uid % 4
                u2 = t2 % 2
                if u2 == 0:
                    transpose128(pn[:, s4 * P:(s4 + 1) * P],
                                 [("pn", s4)],
                                 pta[:, s4 * P:(s4 + 1) * P],
                                 [("pta", s4)], evac_eng="scalar", bank=0)
                else:
                    ts_ = 0
                    pl.op("tensor", lambda e, s4=s4, ts_=ts_: e.transpose(
                        tp_ps[ts_][64:128, :],
                        pn[:, s4 * P:s4 * P + 64], ident[:, :]),
                        r=[("pn", s4), "id"], w=[("tp", ts_)],
                        est=170, gid=("tr", t2, pm, 0), solo=True)
                    pl.op("tensor", lambda e, s4=s4, ts_=ts_: e.transpose(
                        tp_ps[ts_][0:64, :],
                        pn[:, s4 * P + 64:s4 * P + P], ident[:, :]),
                        r=[("pn", s4), "id"], w=[("tp", ts_)],
                        est=170, gid=("tr", t2, pm, 1), solo=True)
                    pl.op("scalar", lambda e, s4=s4, ts_=ts_: e.copy(
                        out=pta[:, s4 * P:(s4 + 1) * P],
                        in_=tp_ps[ts_][:, :]),
                        r=[("tp", ts_)], w=[("pta", s4)], est=250)

            def st_ctx(t2, pm, uid):
                s4 = uid % 4
                u2 = t2 % 2
                cps = uid % 2
                cgid = ("cx", t2, pm)
                u = t2 // 2
                for hp in range(2):
                    h = 2 * pm + hp
                    if u2 == 0:
                        pl.op("tensor", lambda e, hp=hp, h=h, u=u,
                              s4=s4, cps=cps: e.matmul(
                                  cx_ps[cps][hp * 64:hp * 64 + 64, 0:64],
                                  v_tm[:, u * D + h * DH:
                                       u * D + (h + 1) * DH],
                                  pta[:, s4 * P + hp * 64:
                                      s4 * P + hp * 64 + 64],
                                  start=True, stop=True,
                                  skip_group_check=True),
                              r=[("v", u, h // 8), ("pta", s4)],
                              w=[("ctxp", cps)], est=180, gid=cgid,
                              solo=True)
                    else:
                        pl.op("tensor", lambda e, hp=hp, h=h, u=u,
                              s4=s4, cps=cps: e.matmul(
                                  cx_ps[cps][hp * 64:hp * 64 + 64, 0:64],
                                  v_tm[64:128, u * D + h * DH:
                                       u * D + (h + 1) * DH],
                                  pta[64:128, s4 * P + hp * 64:
                                      s4 * P + hp * 64 + 64],
                                  start=True, stop=True,
                                  skip_group_check=True),
                              r=[("v", u, h // 8), ("pta", s4)],
                              w=[("ctxp", cps)], est=180, gid=cgid,
                              solo=True)
                        pl.op("tensor", lambda e, hp=hp, h=h, u=u,
                              s4=s4, cps=cps: e.matmul(
                                  cx_ps[1 - cps][hp * 64:hp * 64 + 64, 0:64],
                                  v_tm[0:64, (u + 1) * D + h * DH:
                                       (u + 1) * D + (h + 1) * DH],
                                  pta[0:64, s4 * P + hp * 64:
                                      s4 * P + hp * 64 + 64],
                                  start=True, stop=True,
                                  skip_group_check=True),
                              r=[("v", u + 1, h // 8), ("pta", s4)],
                              w=[("ctxp", 1 - cps)], est=180, gid=cgid,
                              solo=True)

            def st_evac(t2, pm, uid):
                u2 = t2 % 2
                cps = uid % 2
                if u2 == 0:
                    pl.op("scalar", lambda e, pm=pm, cps=cps: e.copy(
                        out=ctxfm[:, pm * P:pm * P + 64],
                        in_=cx_ps[cps][:, 0:64]),
                        r=[("ctxp", cps)],
                        w=[("cfm", pm, 0)], est=200)
                else:
                    pl.op("scalar", lambda e, pm=pm, cps=cps: e.copy(
                        out=ctxfm[:, pm * P + 64:pm * P + 128],
                        in_=cx_ps[cps][:, 0:64]),
                        r=[("ctxp", cps)],
                        w=[("cfm", pm, 1)], est=200)
                    pl.op("vector", lambda e, pm=pm, cps=cps: e.tensor_tensor(
                        out=ctxfm[:, pm * P + 64:pm * P + 128],
                        in0=cx_ps[1 - cps][:, 0:64],
                        in1=ctxfm[:, pm * P + 64:pm * P + 128],
                        op=ALU.add),
                        r=[("ctxp", 1 - cps), ("cfm", pm, 1)],
                        w=[("cfm", pm, 1)], est=220)

            stages = [(5, st_evac), (4, st_ctx), (3, st_tr), (2, st_norm),
                      (1, st_soft), (0, st_scores)]
            NU = len(units)
            for step in range(NU + 6):
                ui_done = step - 6
                if 0 <= ui_done < NU and ui_done % 16 == 15:
                    emit_tile_tail(c, 4 * c + ui_done // 16)
                for lag, fn_ in stages:
                    ui = step - lag
                    if 0 <= ui < NU:
                        t2, pm = units[ui]
                        fn_(t2, pm, ui)
        pl.op("vector", lambda e: e.memset(col(C_ABS), 0.0), w=[("c", C_ABS)],
              est=100)
        for kc in range(CH):
            sl = kc % 2
            stg = wst[:, sl * D:(sl + 1) * D]
            pl.op("sync", lambda e, stg=stg, kc=kc: e.dma_start(
                out=stg, in_=wf_d[kc * P:(kc + 1) * P, :]),
                w=[("wst", sl)], dma=True, dsem=f"wst{sl}", est=2400)
            pl.op("vector", lambda e, stg=stg: e.tensor_reduce(
                out=col(C_TMP), in_=stg, axis=AX.X, op=ALU.add,
                apply_absolute_value=True),
                r=[("wst", sl)], w=[("c", C_TMP)], est=1200)
            pl.op("vector", lambda e: e.tensor_tensor(
                out=col(C_ABS), in0=col(C_ABS), in1=col(C_TMP), op=ALU.add),
                r=[("c", C_ABS), ("c", C_TMP)], w=[("c", C_ABS)], est=170)
        pl.op("tensor", lambda e: e.matmul(
            cx_ps[0][0:1, 0:1], col(C_ABS), ones_cf[:, :], start=True, stop=True,
            skip_group_check=True),
            r=[("c", C_ABS), "oncf"],
            w=[("ctxp", 0)], est=300, solo=True)
        pl.op("vector", lambda e: e.tensor_scalar(
            out=sc(S_BETA), in0=cx_ps[0][0:1, 0:1], scalar1=1.0 / (D * D),
            scalar2=None, op0=ALU.mult),
            r=[("ctxp", 0)], w=[("s", S_BETA)], est=170)
        pl.op("vector", lambda e: e.tensor_scalar(
            out=sc(S_INV), in0=sc(S_BETA), scalar1=EPS, scalar2=None,
            op0=ALU.add), r=[("s", S_BETA)], w=[("s", S_INV)], est=170)
        pl.op("vector", lambda e: e.tensor_scalar(
            out=sc(S_INV), in0=sc(S_INV), scalar1=0.5, scalar2=None,
            op0=ALU.mult), r=[("s", S_INV)], w=[("s", S_INV)], est=170)
        pl.op("tensor", lambda e: e.matmul(
            cx_ps[0][:, 0:1], ones_rf[:1, :], sc(S_INV), start=True, stop=True,
            skip_group_check=True),
            r=["onrf", ("s", S_INV)],
            w=[("ctxp", 0)], est=300, solo=True)
        pl.op("vector", lambda e: e.tensor_copy(
            out=col(C_INVB), in_=cx_ps[0][:, 0:1]),
            r=[("ctxp", 0)], w=[("c", C_INVB)], est=170)
        pl.op("vector", lambda e: e.tensor_scalar(
            out=col(59), in0=col(C_INVB), scalar1=-1.0, scalar2=None,
            op0=ALU.mult), r=[("c", C_INVB)], w=[("c", 59)], est=120)
        for kc in range(CH):
            sl = kc % 2
            stg = wst[:, sl * D:(sl + 1) * D]
            tl = tmp_ln[:, sl * D:(sl + 1) * D]
            ao = aosb[:, sl * D:(sl + 1) * D]
            pl.op("sync", lambda e, stg=stg, kc=kc: e.dma_start(
                out=stg, in_=wf_d[kc * P:(kc + 1) * P, :]),
                w=[("wst", sl)], dma=True, dsem=f"wst{sl}", est=2400)
            pl.op("vector", lambda e, stg=stg, ao=ao, sl=sl: e.tensor_scalar(
                out=ao, in0=stg, scalar1=col(C_INVB), scalar2=None,
                op0=ALU.is_gt), r=[("wst", sl), ("c", C_INVB)],
                w=[("ao", sl, 0), ("ao", sl, 1)], est=2000)
            pl.op("vector", lambda e, stg=stg, tl=tl, sl=sl: e.tensor_scalar(
                out=tl, in0=stg, scalar1=col(59), scalar2=None,
                op0=ALU.is_lt), r=[("wst", sl), ("c", 59)],
                w=[("tl", sl, 0), ("tl", sl, 1)], est=2000)
            pl.op("vector", lambda e, kc=kc, tl=tl, ao=ao, sl=sl: e.tensor_tensor(
                out=wa[:, kc * D:(kc + 1) * D], in0=ao, in1=tl,
                op=ALU.subtract),
                r=[("ao", sl, 0), ("ao", sl, 1), ("tl", sl, 0),
                   ("tl", sl, 1)],
                w=[("wa", kc)], est=2400)
        idw = wst[:, 0:P]
        pl.op("gpsimd", lambda e, idw=idw: e.memset(idw, 0.0),
              r=[("wst", 0)], w=[("wst", 0)], est=150)
        pl.op("gpsimd", lambda e, idw=idw: e.affine_select(
            out=idw, in_=idw, compare_op=ALU.not_equal,
            fill=1.0, base=0, pattern=[[-1, P]], channel_multiplier=1),
            r=[("wst", 0)], w=[("wst", 0)], est=400)
        pl.op("tensor", lambda e, idw=idw: e.transpose(
            cx_ps[0][0:1, :], col(C_GMAX), idw),
            r=[("c", C_GMAX), ("wst", 0)], w=[("ctxp", 0)], est=300,
            solo=True)
        pl.op("vector", lambda e: e.tensor_reduce(
            out=sc(S_GAM), in_=cx_ps[0][0:1, 0:P], axis=AX.X, op=ALU.max),
            r=[("ctxp", 0)], w=[("s", S_GAM)], est=220)
        pl.op("sync", lambda e: e.dma_start(
            out=cc_in.ap(), in_=sc(S_GAM)), r=[("s", S_GAM)], dma=True,
            dsem="ccw", est=1500)
        pl.op("gpsimd", lambda e: e.collective_compute(
            "AllReduce", ALU.max, replica_groups=[list(range(8))],
            ins=[cc_in.ap()], outs=[cc_out.ap()]),
            deps=[len(pl.ops) - 1], cc=True)
        pl.op("sync", lambda e: e.dma_start(
            out=sc(S_GAM), in_=cc_out.ap()),
            w=[("s", S_GAM)], deps=[len(pl.ops) - 1], dma=True, dsem="ccr",
            est=1500)
        pl.op("vector", lambda e: e.reciprocal(out=sc(S_S1), in_=sc(S_GAM)),
              r=[("s", S_GAM)], w=[("s", S_S1)], est=170)
        pl.op("vector", lambda e: e.tensor_scalar(
            out=sc(S_S1), in0=sc(S_S1), scalar1=QB, scalar2=None, op0=ALU.mult),
            r=[("s", S_S1)], w=[("s", S_S1)], est=170)
        pl.op("vector", lambda e: e.tensor_tensor(
            out=sc(S_S2), in0=sc(S_GAM), in1=sc(S_BETA), op=ALU.mult),
            r=[("s", S_GAM), ("s", S_BETA)], w=[("s", S_S2)], est=170)
        pl.op("vector", lambda e: e.tensor_scalar(
            out=sc(S_S2), in0=sc(S_S2), scalar1=1.0 / QB, scalar2=None,
            op0=ALU.mult), r=[("s", S_S2)], w=[("s", S_S2)], est=170)
        pl.op("tensor", lambda e: e.matmul(
            cx_ps[0][:, 0:1], ones_rf[:1, :], sc(S_S1), start=True, stop=True,
            skip_group_check=True),
            r=["onrf", ("s", S_S1)], w=[("ctxp", 0)],
            est=300, solo=True)
        pl.op("vector", lambda e: e.tensor_copy(
            out=col(C_S1B), in_=cx_ps[0][:, 0:1]),
            r=[("ctxp", 0)], w=[("c", C_S1B)], est=170)
        pl.op("tensor", lambda e: e.matmul(
            cx_ps[1][:, 0:1], ones_rf[:1, :], sc(S_S2), start=True, stop=True,
            skip_group_check=True),
            r=["onrf", ("s", S_S2)], w=[("ctxp", 1)],
            est=300, solo=True)
        pl.op("vector", lambda e: e.tensor_copy(
            out=col(C_S2B), in_=cx_ps[1][:, 0:1]),
            r=[("ctxp", 1)], w=[("c", C_S2B)], est=170)

        CLIP = QB - EPS
        toks = [None]

        def b2_clip(g):
            s4 = g % 4
            qs = qfm[:, s4 * D:(s4 + 1) * D]
            pl.op("vector", lambda e, g=g, qs=qs: e.tensor_scalar(
                out=qs, in0=v_tm[:, g * D:(g + 1) * D],
                scalar1=col(C_S1B), scalar2=CLIP, op0=ALU.mult, op1=ALU.min),
                r=[("v", g, 0), ("v", g, 1), ("c", C_S1B)], w=[("qf", s4)],
                est=1300)
            pl.op("vector", lambda e, qs=qs: e.tensor_scalar(
                out=qs, in0=qs, scalar1=-CLIP, scalar2=None, op0=ALU.max),
                r=[("qf", s4)], w=[("qf", s4)], est=1100)

        def b2_dst(g):
            s4 = g % 4
            buf = aosb if s4 < 2 else wst
            return buf[:, (s4 % 2) * D:(s4 % 2 + 1) * D], s4

        def b2_xdma(g):
            xsl = g % 2
            xa = xbuf[:, xsl * D:(xsl + 1) * D]
            pl.op("sync", lambda e, g=g, xa=xa: e.dma_start(
                out=xa, in_=xnew_d[g * P:(g + 1) * P, :]),
                r=[("xnd", g)], w=[("xb", xsl)], dma=True, dsem=f"xb{xsl}",
                est=2400)

        def b2_mm(g, nh):
            s4 = g % 4
            dst, _ = b2_dst(g)
            wkeys = [("b2o", s4, nh)] + ([("wst", s4 % 2)] if s4 >= 2 else [])
            mm_group(
                lambda m2, s4=s4: qfm[:, s4 * D + m2 * P:
                                      s4 * D + (m2 + 1) * P],
                lambda m2, nh=nh: wa[:, m2 * D + nh * 512:
                                     m2 * D + (nh + 1) * 512],
                lambda m2, s4=s4: [("wa", m2), ("qf", s4)],
                lambda e, src, nh=nh, dst=dst: e.activation(
                    out=dst[:, nh * 512:(nh + 1) * 512], in_=src,
                    func=AF.Copy, scale=col(C_S2B)),
                [("c", C_S2B)], wkeys,
                evac_eng="scalar", evac_est=700)

        def b2_ln(g):
            dst, s4 = b2_dst(g)
            ln_stats(dst, [("b2o", s4, 0), ("b2o", s4, 1)], LN_EPS_OUT, s4)
            tl = tmp_ln[:, (g % 2) * D:(g % 2 + 1) * D]
            ln_apply(tl, [("tl", g % 2, 0), ("tl", g % 2, 1)], dst,
                     [("b2o", s4, 0), ("b2o", s4, 1)], s4, eng="vector",
                     est=1300)

        def b2_out(g):
            xsl = g % 2
            tl = tmp_ln[:, xsl * D:(xsl + 1) * D]
            xa = xbuf[:, xsl * D:(xsl + 1) * D]
            pl.op("vector", lambda e, xa=xa, tl=tl: e.tensor_tensor(
                out=xa, in0=xa, in1=tl, op=ALU.add),
                r=[("xb", xsl), ("tl", xsl, 0), ("tl", xsl, 1)],
                w=[("xb", xsl)], est=2000)
            toks[0] = pl.op("gpsimd", lambda e, g=g, xa=xa: e.dma_start(
                out=y_d[g * P:(g + 1) * P, :], in_=xa),
                r=[("xb", xsl)], w=["ywd"], dma=True, dsem="yw", est=2400)

        b2_stages = [(3, b2_out), (2, b2_ln),
                     (1, lambda g: (b2_mm(g, 0), b2_mm(g, 1))),
                     (1, b2_xdma), (0, b2_clip)]
        for step in range(NT + 3):
            for lag, fn_ in b2_stages:
                g = step - lag
                if 0 <= g < NT:
                    fn_(g)
        tok = toks[0]

        pl.op("gpsimd", lambda e: e.engine_nop(), deps=[tok], est=50)

        pl.emit(nc, st)
        print(f"[kernel] planned {len(pl.ops)} ops, "
              f"est span {pl.est_span / 1e3:.1f} us")
    return nc


_NC_CACHE = None


def kernel(L0_attn_x, L0_attn_Wq, L0_attn_Wk, L0_attn_Wv, L0_attn_Wo,
           L0_ffn_weight, L0_ffn_ln_gamma, L0_ffn_ln_beta):
    global _NC_CACHE
    x = np.ascontiguousarray(np.asarray(L0_attn_x, dtype=np.float32))
    wq = np.ascontiguousarray(np.asarray(L0_attn_Wq, dtype=np.float32))
    wk = np.ascontiguousarray(np.asarray(L0_attn_Wk, dtype=np.float32))
    wv = np.ascontiguousarray(np.asarray(L0_attn_Wv, dtype=np.float32))
    wo = np.ascontiguousarray(np.asarray(L0_attn_Wo, dtype=np.float32))
    wf = np.ascontiguousarray(np.asarray(L0_ffn_weight, dtype=np.float32))
    gam = np.ascontiguousarray(np.asarray(L0_ffn_ln_gamma, dtype=np.float32)).reshape(1, D)
    bet = np.ascontiguousarray(np.asarray(L0_ffn_ln_beta, dtype=np.float32)).reshape(1, D)

    B, L, _ = x.shape
    in_maps = []
    for c in range(8):
        b, h = c // 2, c % 2
        seq = x[b]
        halo = np.zeros((P, D), np.float32) if h == 0 else \
            np.ascontiguousarray(seq[h * T - P:h * T])
        xc = np.concatenate([halo, seq[h * T:(h + 1) * T]], axis=0)
        flag = np.full((P, 1), 1.0 if h == 0 else 0.0, np.float32)
        in_maps.append({"x": xc, "wq": wq, "wk": wk, "wv": wv, "wo": wo,
                        "wf": wf, "gam": gam, "bet": bet, "flag": flag})

    if _NC_CACHE is None:
        _NC_CACHE = build_nc()
    res = run_bass_kernel_spmd(_NC_CACHE, in_maps, core_ids=list(range(8)))
    out = np.empty((B, L, D), np.float32)
    for c in range(8):
        b, h = c // 2, c % 2
        out[b, h * T:(h + 1) * T, :] = res.results[c]["y"]
    return out
